# version 02 (final)
# baseline (speedup 1.0000x reference)
import numpy as np
import ml_dtypes
from contextlib import ExitStack

import concourse.bass as bass
import concourse.mybir as mybir
from concourse.bass_utils import run_bass_kernel_spmd

F32 = mybir.dt.float32
BF16 = mybir.dt.bfloat16
AF = mybir.ActivationFunctionType
ALU = mybir.AluOpType
AX = mybir.AxisListType

BF16NP = ml_dtypes.bfloat16

NCORE = 8
B = 2
S = 4096
D = 2048
TB = S // NCORE
T = B * TB
NT = T // 128
KC = D // 128
HD = 128
NH = 16
INVSQ = float(1.0 / np.sqrt(128.0))

ENGINES = ("sync", "gpsimd", "scalar", "vector", "tensor")


class Core:

    def __init__(self, nc):
        self.nc = nc
        self.es = ExitStack()
        self.q = {e: [] for e in ENGINES}
        self._sems = {}
        self.meta = {}
        self.records = []
        for qn in ("sync", "gpsimd", "scalar"):
            self.sem("dq_" + qn)

    def sem(self, name):
        h = self.es.enter_context(self.nc.semaphore(name))
        self._sems[name] = [h, 0]
        return name

    def planned(self, name):
        return self._sems[name][1]

    def _h(self, name):
        return self._sems[name][0]

    def sb(self, name, shape, dtype):
        return self.es.enter_context(self.nc.sbuf_tensor(name, shape, dtype))

    def sba(self, name, shape, dtype, offset):
        return self.nc.alloc_sbuf_tensor_at(name, shape, dtype, offset=offset)

    def ps(self, name, shape, dtype=F32):
        return self.es.enter_context(self.nc.psum_tensor(name, shape, dtype))

    def step(self, engine, emit, waits=(), incs=()):
        wlist = [(self._h(s), int(v)) for (s, v) in waits]
        ilist = []
        self.meta.setdefault(engine, []).append(
            ([(s, int(v)) for (s, v) in waits], [(s, n) for (s, n) in incs]))
        for (s, n) in incs:
            self._sems[s][1] += n
            ilist.append((self._h(s), n))

        rec = {"engine": engine, "idx": len(self.q[engine]), "waits": [(s, int(v)) for (s, v) in waits],
               "incs": [(s, n) for (s, n) in incs], "args": None, "kind": None}
        self.records.append(rec)

        def fn(eng, emit=emit, wlist=wlist, ilist=ilist, rec=rec):
            for h, v in wlist:
                eng.wait_ge(h, v)
            inst = emit(eng) if emit is not None else None
            if inst is not None:
                mi = inst.ins
                rec["kind"] = type(mi).__name__
                rec["args"] = ([(True, a) for a in mi.outs if hasattr(a, "ap")] +
                               [(False, a) for a in mi.ins if hasattr(a, "ap")])
            if ilist:
                if inst is None:
                    inst = eng.nop()
                for h, n in ilist:
                    inst = inst.then_inc(h, n)

        self.q[engine].append(fn)

    def dma(self, queue, out, in_, waits=(), **kw):
        self.step(queue, lambda eng: eng.dma_start(out=out, in_=in_, **kw),
                  waits=waits, incs=[("dq_" + queue, 16)])
        return self.planned("dq_" + queue)

    def qcount(self, queue):
        return self.planned("dq_" + queue)

    def check_schedule(self):
        pos = {e: 0 for e in ENGINES}
        counts = {s: 0 for s in self._sems}
        total = sum(len(self.meta.get(e, [])) for e in ENGINES)
        done = 0
        progress = True
        while progress:
            progress = False
            for e in ENGINES:
                q = self.meta.get(e, [])
                while pos[e] < len(q):
                    waits, incs = q[pos[e]]
                    if any(counts[s] < v for (s, v) in waits):
                        break
                    for (s, n) in incs:
                        counts[s] += n
                    pos[e] += 1
                    done += 1
                    progress = True
        if done != total:
            for e in ENGINES:
                q = self.meta.get(e, [])
                if pos[e] < len(q):
                    waits, incs = q[pos[e]]
                    blocked = [(s, v, counts[s]) for (s, v) in waits if counts[s] < v]
                    print(f"DEADLOCK: engine {e} at step {pos[e]}/{len(q)} blocked on {blocked}")
            raise RuntimeError(f"schedule deadlock: {done}/{total} steps executable")

    def check_races(self, max_report=20):
        import bisect
        nc = self.nc
        lanes = list(ENGINES)
        lane_idx = {e: i for i, e in enumerate(lanes)}
        produced = {s: [] for s in self._sems}
        nodes = []
        def mloc_info(name):
            try:
                ml = self._mlocs[name]
                return ml
            except KeyError:
                return None
        self._mlocs = {}
        for t in nc.m.functions[0].allocations:
            if hasattr(t, "memorylocations") and t.memorylocations:
                ml = t.memorylocations[0]
                self._mlocs[ml.name] = ml
        def footprints(args):
            fps = []
            for is_w, a in args:
                name = a.tensor if isinstance(getattr(a, "tensor", None), str) else getattr(getattr(a, "tensor", None), "name", None)
                if name is None:
                    name = getattr(a, "memref", None)
                    name = getattr(name, "name", name)
                ml = self._mlocs.get(name)
                esz = mybir.dt.size(a.dtype)
                ap = list(a.ap)
                off = a.offset
                if ml is not None and getattr(ml, "Skind", None) is not None:
                    pass
                space = None
                base = 0
                if ml is not None:
                    tp = str(getattr(ml, "type", ""))
                    if "SB" in tp.upper():
                        space = "sbuf"
                        base = ml.addr
                    elif "PSUM" in tp.upper() or "PSum" in tp:
                        space = "psum"
                        base = ml.bank * 2048 + ml.addr
                if space is None:
                    span = sum(st * (cnt - 1) for st, cnt in ap) if ap else 0
                    fps.append((is_w, "dram:" + str(name), off * esz, (off + span + 1) * esz))
                    continue
                if len(ap) >= 1:
                    rowlen = ap[0][0]
                    f_off = off % rowlen if rowlen else off
                    span = sum(st * (cnt - 1) for st, cnt in ap[1:]) if len(ap) > 1 else 0
                else:
                    f_off, span = off, 0
                lo = base + f_off * esz
                hi = base + (f_off + span + 1) * esz
                fps.append((is_w, space, lo, hi))
            return fps
        clocks = []
        NL = len(lanes)
        poscount = [0] * NL
        for rec in self.records:
            li = lane_idx[rec["engine"]]
            node = {"lane": li, "pos": poscount[li], "waits": rec["waits"],
                    "fps": footprints(rec["args"]) if rec["args"] else [],
                    "kind": rec["kind"], "gidx": len(nodes)}
            poscount[li] += 1
            clk = [-1] * NL
            prev = node["pos"] - 1
            if prev >= 0:
                pclk = clocks[self._lane_nodes[li][prev]]
                clk = list(pclk)
            for (s, v) in rec["waits"]:
                hist = produced[s]
                j = bisect.bisect_left(hist, (v, -1))
                if j < len(hist):
                    pid = hist[j][1]
                    pclk = clocks[pid]
                    for k in range(NL):
                        if pclk[k] > clk[k]:
                            clk[k] = pclk[k]
            clk[li] = node["pos"]
            nodes.append(node)
            clocks.append(clk)
            self._lane_nodes.setdefault(li, []) if False else None
            self._lane_nodes[li].append(node["gidx"])
            for (s, n) in rec["incs"]:
                cum = (produced[s][-1][0] if produced[s] else 0) + n
                produced[s].append((cum, node["gidx"]))
        from collections import defaultdict
        by_space = defaultdict(list)
        for nd in nodes:
            for (is_w, sp, lo, hi) in nd["fps"]:
                by_space[sp].append((lo, hi, is_w, nd["gidx"]))
        nrep = 0
        for sp, items in by_space.items():
            items.sort()
            active = []
            for (lo, hi, is_w, gid) in items:
                active = [a for a in active if a[1] > lo]
                for (alo, ahi, aw, agid) in active:
                    if not (aw or is_w):
                        continue
                    a, b = nodes[agid], nodes[gid]
                    if a["lane"] == b["lane"]:
                        continue
                    ca, cb = clocks[agid], clocks[gid]
                    hb_ab = cb[a["lane"]] >= a["pos"]
                    hb_ba = ca[b["lane"]] >= b["pos"]
                    if not (hb_ab or hb_ba):
                        nrep += 1
                        if nrep <= max_report:
                            print(f"RACE {sp}: [{max(lo,alo)},{min(hi,ahi)}) "
                                  f"{lanes[a['lane']]}#{a['pos']}({a['kind']},{'W' if aw else 'R'}) vs "
                                  f"{lanes[b['lane']]}#{b['pos']}({b['kind']},{'W' if is_w else 'R'})")
                active.append((lo, hi, is_w, gid))
        print(f"race check: {nrep} unordered conflicting pairs over {len(nodes)} steps")
        return nrep

    def emit(self):
        self._lane_nodes = {i: [] for i in range(len(ENGINES))}
        self.check_schedule()
        block = self.es.enter_context(self.nc.Block())
        for name in ENGINES:
            fns = self.q[name]

            def body(eng, fns=fns):
                for f in fns:
                    f(eng)

            getattr(block, name)(body)
        self.check_races()
        self.es.close()


def transpose_seq(P, pieces, banks, sident, pre_waits, evac):
    base_v = P.planned("v")
    nb = len(banks)
    for idx, src in enumerate(pieces):
        waits = []
        if idx == 0:
            waits = list(pre_waits)
            waits.append(("v", base_v))
        elif idx >= nb:
            waits.append(("v", base_v + idx - nb + 1))
        P.step("tensor",
               lambda eng, src=src, bank=banks[idx % nb]: eng.transpose(
                   out=bank[:, 0:128], in_=src, identity=sident[:]),
               waits=waits, incs=[("t", 1)])
        evac(idx, banks[idx % nb][:, 0:128], P.planned("t"))


def emit_layernorm_inplace(P, x_tiles, scratch, tag, out_bf=None):
    P.sem("v_" + tag)
    P.sem("s_" + tag)
    sv, ss = "v_" + tag, "s_" + tag
    nsum, rstd, xc = scratch
    for i, xt in enumerate(x_tiles):
        P.step("vector",
               lambda eng, xt=xt: eng.reduce_sum(out=nsum[:], in_=xt[:], axis=AX.X,
                                                 negate=True),
               waits=[(ss, P.planned(ss)), (sv, P.planned(sv))])
        P.step("vector", lambda eng: eng.tensor_copy(out=xc[:, 0:64], in_=xc[:, 0:64]))
        P.step("vector",
               lambda eng: eng.tensor_scalar(out=nsum[:], in0=nsum[:],
                                             scalar1=1.0 / D, scalar2=None,
                                             op0=ALU.mult))
        P.step("vector", lambda eng: eng.tensor_copy(out=xc[:, 0:64], in_=xc[:, 0:64]))
        P.step("vector",
               lambda eng, xt=xt: eng.tensor_tensor(
                   out=xc[:], in0=xt[:], in1=nsum[:, 0:1].broadcast_to([128, D]),
                   op=ALU.add),
               incs=[(sv, 1)])
        P.step("scalar",
               lambda eng: eng.activation(out=xc[:], in_=xc[:], func=AF.Square,
                                          accum_out=rstd[:]),
               waits=[(sv, P.planned(sv))])
        P.step("scalar",
               lambda eng: eng.activation(out=xc[:, 0:256], in_=xc[:, 0:256],
                                          func=AF.Square),
               incs=[(ss, 1)])
        P.step("vector",
               lambda eng: eng.tensor_scalar(out=rstd[:], in0=rstd[:],
                                             scalar1=1.0 / D, scalar2=1e-5,
                                             op0=ALU.mult, op1=ALU.add),
               waits=[(ss, P.planned(ss))], incs=[(sv, 1)])
        P.step("scalar",
               lambda eng: eng.activation(out=rstd[:], in_=rstd[:], func=AF.Ln),
               waits=[(sv, P.planned(sv))])
        P.step("scalar", lambda eng: eng_spacer(eng), incs=[(ss, 1)])
        P.step("vector",
               lambda eng: eng.tensor_scalar(out=rstd[:], in0=rstd[:],
                                             scalar1=-0.5, scalar2=None,
                                             op0=ALU.mult),
               waits=[(ss, P.planned(ss))], incs=[(sv, 1)])
        P.step("scalar",
               lambda eng: eng.activation(out=rstd[:], in_=rstd[:], func=AF.Exp),
               waits=[(sv, P.planned(sv))])
        P.step("scalar", lambda eng: eng_spacer(eng), incs=[(ss, 1)])
        P.step("vector",
               lambda eng, xt=xt: eng.tensor_tensor(
                   out=xt[:], in0=xt[:], in1=nsum[:, 0:1].broadcast_to([128, D]),
                   op=ALU.add),
               waits=[(ss, P.planned(ss))])
        P.step("vector",
               lambda eng, xt=xt: eng.tensor_tensor(
                   out=xt[:], in0=xt[:], in1=rstd[:, 0:1].broadcast_to([128, D]),
                   op=ALU.mult),
               incs=[(sv, 1)])
        if out_bf is not None:
            P.step("vector",
                   lambda eng, xt=xt, ob=out_bf[i]: eng.tensor_copy(out=ob[:], in_=xt[:]),
                   incs=[(sv, 1)])
    return sv


_SPACER_TILE = [None]


def eng_spacer(eng):
    t = _SPACER_TILE[0]
    return eng.activation(out=t, in_=t, func=AF.Copy)


def _cut(P, x_sb, x1, extra_waits=()):
    for i in range(NT):
        P.dma("sync", x1[i * 128:(i + 1) * 128, :], x_sb[i][:],
              waits=list(extra_waits) if i == 0 else [])
    P.step("sync", None, waits=[("dq_sync", P.qcount("sync"))])
    P.emit()


def _dump(P, dbg, ap, cols, x_sb, x1):
    P.dma("gpsimd", dbg[:, 0:cols], ap,
          waits=[("v", P.planned("v")), ("s", P.planned("s")), ("t", P.planned("t")),
                 ("dq_sync", P.qcount("sync")), ("dq_scalar", P.qcount("scalar"))])
    P.step("gpsimd", None, waits=[("dq_gpsimd", P.qcount("gpsimd"))])
    _cut(P, x_sb, x1, extra_waits=[("dq_gpsimd", P.qcount("gpsimd"))])


def build_l0_attn(stop=None):
    nc = bass.Bass("TRN2", target_bir_lowering=False, debug=False)
    NKV = 4

    xs = nc.dram_tensor("xs", [T, D], F32, kind="ExternalInput").ap()
    wq = nc.dram_tensor("wq", [D, D], F32, kind="ExternalInput").ap()
    wk = nc.dram_tensor("wk", [512, D], F32, kind="ExternalInput").ap()
    wv = nc.dram_tensor("wv", [512, D], F32, kind="ExternalInput").ap()
    wo = nc.dram_tensor("wo", [D, D], F32, kind="ExternalInput").ap()
    cosf = nc.dram_tensor("cosf", [128, T], F32, kind="ExternalInput").ap()
    sinf = nc.dram_tensor("sinf", [128, T], F32, kind="ExternalInput").ap()
    ident = nc.dram_tensor("ident", [128, 128], BF16, kind="ExternalInput").ap()
    identf = nc.dram_tensor("identf", [128, 128], F32, kind="ExternalInput").ap()
    x1 = nc.dram_tensor("x1out", [T, D], F32, kind="ExternalOutput").ap()
    dbg = nc.dram_tensor("dbg", [128, 16384], F32, kind="ExternalOutput").ap()
    dbgh = nc.dram_tensor("dbgh", [128, 16384], BF16, kind="ExternalOutput").ap()

    kgin = nc.dram_tensor("kgin", [B * NKV * 128, TB], BF16)
    kg = nc.dram_tensor("kgather", [NCORE * B * NKV * 128, TB], BF16)
    vgin = nc.dram_tensor("vgin", [B * NKV * TB, 128], BF16)
    vg = nc.dram_tensor("vgather", [NCORE * B * NKV * TB, 128], BF16)

    P = Core(nc)

    o = [16640]
    def take(n):
        off = o[0]; o[0] += (n + 63) // 64 * 64
        return off
    x_sb = [P.sba(f"xt{i}", [128, D], F32, take(D * 4)) for i in range(NT)]
    sident = P.sba("sident", [128, 128], BF16, take(256))
    sidentf = P.sba("sidentf", [128, 128], F32, take(512))
    wbuf = [P.sba(f"wbuf{i}", [128, KC * 128], BF16, take(KC * 128 * 2)) for i in range(2)]
    qbf = P.sba("qbf", [128, NH * T], BF16, take(NH * T * 2))
    ctx = P.sba("ctx", [128, NH * TB], BF16, take(NH * TB * 2))
    for nm in ("nmax", "ssum", "rsum", "ln_nsum", "ln_rstd"):
        pass
    nmax = P.sba("nmax", [128, 1], F32, take(64))
    ssum = P.sba("ssum", [128, 1], F32, take(64))
    rsum = P.sba("rsum", [128, 1], F32, take(64))
    ln_nsum = P.sba("ln_nsum", [128, 1], F32, take(64))
    ln_rstd = P.sba("ln_rstd", [128, 1], F32, take(64))
    A = take(0)
    xF = P.sba("xF", [128, KC * T], BF16, A)
    xbf = P.sba("xbf", [128, D], BF16, A + 32768)
    scos = P.sba("scos", [128, T], F32, A + 36864)
    ssin = P.sba("ssin", [128, T], F32, A + 40960)
    pr_f = P.sba("pr_f", [128, T], F32, A + 45056)
    pr_r = P.sba("pr_r", [128, T], F32, A + 49152)
    pr_t = P.sba("pr_t", [128, T], F32, A + 53248)
    kvbf = P.sba("kvbf", [128, T], BF16, A + 57344)
    vt_bf = P.sba("vt_bf", [128, 1024], BF16, A + 59392)
    probsT = P.sba("probsT", [128, 4 * S], BF16, A)
    scores = P.sba("scores", [128, S], F32, A + 32768)
    probs = P.sba("probs", [128, S], BF16, A + 49152)
    ksb = P.sba("ksb", [128, S], BF16, A + 57344)
    vtsb = P.sba("vtsb", [128, S], BF16, A + 65536)
    ln_xc = P.sba("ln_xc", [128, D], F32, A + 32768)
    ot_f = P.sba("ot_f", [128, 512], F32, A + 73728)
    ot_t = P.sba("ot_t", [128, 512], F32, A + 75776)
    wstage = P.sba("wstage", [128, KC * 128], F32, A + 77824)
    flush = P.sba("flush", [128, 256], BF16, A + 86016)
    _SPACER_TILE[0] = flush[:, 128:192]
    lsum = P.sba("lsum", [128, 1], F32, A + 86528)
    rsum2 = P.sba("rsum2", [128, 1], F32, A + 86592)
    assert A + 86656 <= 229300, f"sbuf overflow {A + 86656}"

    tp_a = P.ps("tp_a", [128, 1024], BF16)
    tp_b = P.ps("tp_b", [128, 1024], BF16)
    tf_a = P.ps("tf_a", [128, 512], F32)
    tf_b = P.ps("tf_b", [128, 512], F32)
    sc_ps = P.ps("sc_ps", [128, 1024], F32)
    ctx_ps = P.ps("ctx_ps", [128, 512], F32)
    pj_ps = P.ps("pj_ps", [128, 512], F32)

    P.sem("t")
    P.sem("v")
    P.sem("s")
    P.sem("cc")
    P.sem("g")

    P.step("scalar", lambda eng: eng.memzero(nmax[:]))
    P.step("scalar", lambda eng: eng.activation(out=nmax[:], in_=nmax[:], func=AF.Square))
    P.step("scalar", lambda eng: eng.activation(out=nmax[:], in_=nmax[:], func=AF.Exp))
    P.step("scalar", lambda eng: eng.activation(out=nmax[:], in_=nmax[:], func=AF.Ln),
           incs=[("s", 1)])
    P.step("scalar", lambda eng: eng.memzero(nmax[:]), incs=[("s", 1)])

    P.dma("sync", sident[:], ident)
    P.dma("sync", sidentf[:], identf)
    P.dma("sync", scos[:], cosf)
    P.dma("sync", ssin[:], sinf)
    for i in range(NT):
        P.dma("sync", x_sb[i][:], xs[i * 128:(i + 1) * 128, :])
    loads_done = P.qcount("sync")
    if stop == "loads":
        P.step("sync", None, waits=[("dq_sync", loads_done)])
        _cut(P, x_sb, x1); return nc

    nt_go = 1 if stop == "xf1" else NT
    for i in range(nt_go):
        P.step("vector",
               lambda eng, i=i: eng.tensor_copy(out=xbf[:], in_=x_sb[i][:]),
               waits=[("dq_sync", loads_done), ("t", P.planned("t"))],
               incs=[("v", 1)])
        pieces = [xbf[:, kc * 128:(kc + 1) * 128] for kc in range(KC)]

        def evac(idx, pap, tgt, i=i):
            P.step("vector",
                   lambda eng, idx=idx, pap=pap, i=i: eng.tensor_copy(
                       out=xF[:, idx * T + i * 128: idx * T + (i + 1) * 128],
                       in_=pap),
                   waits=[("t", tgt)], incs=[("v", 1)])

        transpose_seq(P, pieces, [tp_a, tp_b], sident,
                      pre_waits=[("v", P.planned("v"))], evac=evac)

    xf_v = P.planned("v")
    if stop == "d_xf":
        _dump(P, dbg, xF[:], KC * T, x_sb, x1); return nc
    if stop in ("xf", "xf1"):
        _cut(P, x_sb, x1); return nc

    def load_wblock(wsrc, ot):
        wb = wbuf[ot % 2]
        P.dma("sync", wstage[:], wsrc[ot * 128:(ot + 1) * 128, :],
              waits=[("g", P.planned("g"))])
        P.step("gpsimd",
               lambda eng, wb=wb: eng.tensor_copy(out=wb[:], in_=wstage[:]),
               waits=[("dq_sync", P.qcount("sync")), ("t", P.planned("t"))],
               incs=[("g", 1)])
        return wb, P.planned("g")

    def project_tile(wsrc, ot):
        wb, wready = load_wblock(wsrc, ot)
        for b in range(B):
            P.step("tensor", None,
                   waits=[("g", wready), ("v", xf_v), ("v", P.planned("v"))])
            for kc in range(KC):
                P.step("tensor",
                       lambda eng, kc=kc, b=b, wb=wb: eng.matmul(
                           out=pj_ps[:],
                           lhsT=wb[:, kc * 128:(kc + 1) * 128],
                           rhs=xF[:, kc * T + b * TB: kc * T + (b + 1) * TB],
                           start=(kc == 0), stop=(kc == KC - 1)),
                       incs=([("t", 1)] if kc == KC - 1 else []))
            P.step("vector",
                   lambda eng, b=b: eng.tensor_copy(out=pr_f[:, b * TB:(b + 1) * TB],
                                                    in_=pj_ps[:]),
                   waits=[("t", P.planned("t"))], incs=[("v", 1)])

    def rope_to(dst_ap, scale=None):
        vcur = P.planned("v")
        P.dma("scalar", pr_r[1:128, :], pr_f[0:127, :], waits=[("v", vcur)])
        P.dma("scalar", pr_r[0:1, :], pr_f[127:128, :])
        rolled = P.qcount("scalar")
        P.step("vector",
               lambda eng: eng.tensor_tensor(out=pr_t[:], in0=pr_f[:], in1=scos[:],
                                             op=ALU.mult),
               waits=[("dq_scalar", rolled)])
        P.step("vector",
               lambda eng: eng.tensor_tensor(out=pr_r[:], in0=pr_r[:], in1=ssin[:],
                                             op=ALU.mult))
        P.step("vector",
               lambda eng: eng.tensor_tensor(out=pr_t[:], in0=pr_t[:], in1=pr_r[:],
                                             op=ALU.add))
        if scale is None:
            P.step("vector",
                   lambda eng, dst=dst_ap: eng.tensor_copy(out=dst, in_=pr_t[:]),
                   incs=[("v", 1)])
        else:
            P.step("vector",
                   lambda eng, dst=dst_ap, sc=scale: eng.tensor_scalar(
                       out=dst, in0=pr_t[:], scalar1=sc, scalar2=None, op0=ALU.mult),
                   incs=[("v", 1)])

    if stop == "proj1":
        project_tile(wq, 0)
        _cut(P, x_sb, x1); return nc
    if stop == "d_p0":
        project_tile(wq, 0)
        _dump(P, dbg, pr_f[:], T, x_sb, x1); return nc
    if stop == "d_r0":
        project_tile(wq, 0)
        rope_to(qbf[:, 0:T])
        _dump(P, dbg, qbf[:, 0:T], T, x_sb, x1); return nc
    if stop == "d_rt":
        project_tile(wq, 0)
        rope_to(qbf[:, 0:T])
        _dump(P, dbg, pr_t[:], T, x_sb, x1); return nc
    if stop == "d_roll":
        project_tile(wq, 0)
        vcur = P.planned("v")
        P.dma("scalar", pr_r[1:128, :], pr_f[0:127, :], waits=[("v", vcur)])
        P.dma("scalar", pr_r[0:1, :], pr_f[127:128, :])
        P.step("scalar", None, waits=[("dq_scalar", P.qcount("scalar"))])
        _dump(P, dbg, pr_r[:], T, x_sb, x1); return nc
    if stop == "projr":
        project_tile(wq, 0)
        rope_to(qbf[:, 0:T], scale=INVSQ)
        _cut(P, x_sb, x1); return nc
    for h in range(NH):
        project_tile(wq, h)
        rope_to(qbf[:, h * T:(h + 1) * T], scale=INVSQ)
    if stop == "projq":
        _cut(P, x_sb, x1); return nc
    if stop == "d_q0":
        _dump(P, dbg, qbf[:, 0:T], T, x_sb, x1); return nc
    if stop and stop.startswith("d_q2"):
        project_tile(wq, 0)
        rope_to(qbf[:, 0:T])
        if stop == "d_q2noload":
            for b in range(B):
                P.step("tensor", None,
                       waits=[("v", xf_v), ("v", P.planned("v"))])
                for kc in range(KC):
                    P.step("tensor",
                           lambda eng, kc=kc, b=b: eng.matmul(
                               out=pj_ps[:], lhsT=wbuf[0][:, kc * 128:(kc + 1) * 128],
                               rhs=xF[:, kc * T + b * TB: kc * T + (b + 1) * TB],
                               start=(kc == 0), stop=(kc == KC - 1)),
                           incs=([("t", 1)] if kc == KC - 1 else []))
                P.step("vector",
                       lambda eng, b=b: eng.tensor_copy(out=pr_f[:, b * TB:(b + 1) * TB], in_=pj_ps[:]),
                       waits=[("t", P.planned("t"))], incs=[("v", 1)])
            rope_to(qbf[:, T:2 * T])
        elif stop == "d_q2norope":
            project_tile(wq, 1)
            P.step("vector", lambda eng: eng.tensor_copy(out=qbf[:, T:2 * T], in_=pr_t[:]),
                   incs=[("v", 1)])
        else:
            project_tile(wq, 1)
            rope_to(qbf[:, T:2 * T])
        _dump(P, dbg, qbf[:, 0:2 * T], 2 * T, x_sb, x1); return nc
    if stop == "d_qall":
        _dump(P, dbg, qbf[:], NH * T, x_sb, x1); return nc

    for kv in range(NKV):
        project_tile(wk, kv)
        rope_to(kvbf[:])
        vcur = P.planned("v")
        for b in range(B):
            row0 = (b * NKV + kv) * 128
            P.dma("sync", kgin.ap()[row0:row0 + 128, :], kvbf[:, b * TB:(b + 1) * TB],
                  waits=[("v", vcur)] if b == 0 else [])

    if stop == "projk":
        _cut(P, x_sb, x1); return nc
    for kv in range(NKV):
        project_tile(wv, kv)
        P.step("vector", lambda eng: eng.tensor_copy(out=kvbf[:], in_=pr_f[:]),
               incs=[("v", 1)])
        pieces = [kvbf[:, tt * 128:(tt + 1) * 128] for tt in range(8)]

        def evac(idx, pap, tgt, kv=kv):
            P.step("vector",
                   lambda eng, idx=idx, pap=pap: eng.tensor_copy(
                       out=vt_bf[:, idx * 128:(idx + 1) * 128], in_=pap),
                   waits=[("t", tgt), ("dq_sync", P.qcount("sync"))],
                   incs=[("v", 1)])
            vdone = P.planned("v")
            b, trow = (0, idx) if idx < 4 else (1, idx - 4)
            row0 = (b * 4 + kv) * TB + trow * 128
            P.dma("sync", vgin.ap()[row0:row0 + 128, :],
                  vt_bf[:, idx * 128:(idx + 1) * 128],
                  waits=[("v", vdone)])

        transpose_seq(P, pieces, [tp_a, tp_b], sident,
                      pre_waits=[("v", P.planned("v"))], evac=evac)

    P.step("gpsimd",
           lambda eng: eng.collective_compute(
               "AllGather", ALU.bypass, replica_groups=[list(range(NCORE))],
               ins=[kgin.ap()], outs=[kg.ap()]),
           waits=[("dq_sync", P.qcount("sync"))],
           incs=[("cc", 1)])
    P.step("gpsimd",
           lambda eng: eng.collective_compute(
               "AllGather", ALU.bypass, replica_groups=[list(range(NCORE))],
               ins=[vgin.ap()], outs=[vg.ap()]),
           incs=[("cc", 1)])
    cc_done = P.planned("cc")

    if stop == "gather":
        P.step("sync", None, waits=[("cc", cc_done)])
        _cut(P, x_sb, x1); return nc
    if stop == "d_vg":
        P.step("gpsimd", None, waits=[("cc", cc_done)])
        P.dma("gpsimd", dbg[:, 0:32 * 128].rearrange("p (a d) -> p a d", a=32),
              vg.ap()[0:128 * 32, :].rearrange("(a p) d -> p a d", p=128),
              waits=[("v", P.planned("v")), ("s", P.planned("s")), ("t", P.planned("t"))])
        P.step("gpsimd", None, waits=[("dq_gpsimd", P.qcount("gpsimd"))])
        _cut(P, x_sb, x1); return nc
    if stop == "d_kg":
        P.step("gpsimd", None, waits=[("cc", cc_done)])
        P.dma("gpsimd", dbg[:, 0:16 * TB].rearrange("p (a t) -> p a t", a=16),
              kg.ap()[0:128 * 16, :].rearrange("(a p) t -> p a t", p=128),
              waits=[("v", P.planned("v")), ("s", P.planned("s")), ("t", P.planned("t"))])
        P.step("gpsimd", None, waits=[("dq_gpsimd", P.qcount("gpsimd"))])
        _cut(P, x_sb, x1); return nc

    kg_r = kg.ap().rearrange("(r x p) t -> x p r t", r=NCORE, p=128)
    vg_r = vg.ap().rearrange("(r x tb p) d -> x p r tb d", r=NCORE, tb=4, p=128)

    for b in range(B):
        for kv in range(NKV):
            xk = b * NKV + kv
            P.dma("sync", ksb[:].rearrange("p (r t) -> p r t", r=NCORE), kg_r[xk],
                  waits=[("cc", cc_done), ("t", P.planned("t"))])
            for r in range(NCORE):
                P.dma("sync",
                      vtsb[:, r * 512:(r + 1) * 512].rearrange("p (tb d) -> p tb d", tb=4),
                      vg_r[xk][:, r])
            kv_loaded = P.qcount("sync")

            for qh in range(4):
                h = kv * 4 + qh
                for qt in range(4):
                    qoff = h * T + b * TB + qt * 128
                    for rnd in range(4):
                        P.step("tensor", None,
                               waits=[("dq_sync", kv_loaded), ("v", P.planned("v")),
                                      ("s", P.planned("s"))])
                        for c in range(2):
                            ch = rnd * 2 + c
                            P.step("tensor",
                                   lambda eng, ch=ch, c=c, qoff=qoff: eng.matmul(
                                       out=sc_ps[:, c * 512:(c + 1) * 512],
                                       lhsT=qbf[:, qoff:qoff + 128],
                                       rhs=ksb[:, ch * 512:(ch + 1) * 512],
                                       start=True, stop=True),
                                   incs=([("t", 1)] if c == 1 else []))
                        P.step("vector",
                               lambda eng, rnd=rnd: eng.tensor_copy(
                                   out=scores[:, rnd * 1024:(rnd + 1) * 1024],
                                   in_=sc_ps[:]),
                               waits=[("t", P.planned("t"))], incs=[("v", 1)])
                    P.step("vector",
                           lambda eng: eng.reduce_max(out=nmax[:], in_=scores[:],
                                                      axis=AX.X, negate=True))
                    P.step("vector",
                           lambda eng: eng.tensor_copy(out=rsum[:], in_=nmax[:]),
                           incs=[("v", 1)])
                    P.step("scalar",
                           lambda eng: eng.activation(out=probs[:], in_=scores[:],
                                                      func=AF.Exp, bias=rsum[:],
                                                      accum_out=ssum[:]),
                           waits=[("v", P.planned("v"))])
                    P.step("scalar", lambda eng: eng_spacer(eng), incs=[("s", 1)])
                    P.step("vector",
                           lambda eng: eng.reciprocal(out=rsum2[:], in_=ssum[:]),
                           waits=[("s", P.planned("s"))])
                    P.step("vector",
                           lambda eng: eng.tensor_copy(out=flush[:, 192:256],
                                                       in_=flush[:, 192:256]))
                    P.step("vector",
                           lambda eng: eng.tensor_tensor(
                               out=probs[:], in0=probs[:],
                               in1=rsum2[:, 0:1].broadcast_to([128, S]),
                               op=ALU.mult),
                           incs=[("v", 1)])
                    if stop == "d_scores" and (b, kv, qh, qt) == (0, 0, 0, 0):
                        _dump(P, dbg, scores[:], S, x_sb, x1); return nc
                    if stop == "d_probs" and (b, kv, qh, qt) == (0, 0, 0, 0):
                        _dump(P, dbg, probs[:], S, x_sb, x1); return nc
                    if stop == "d_pTs" and (b, kv, qh, qt) == (0, 0, 0, 0):
                        P.dma("sync", dbg[:, 4096:4097], nmax[:],
                              waits=[("s", P.planned("s"))],
                              allow_slow_non_contiguous=True)
                        P.dma("sync", dbg[:, 4097:4098], ssum[:],
                              allow_slow_non_contiguous=True)
                        P.dma("sync", dbgh[:, 3 * S:3 * S + 256], probs[:, 0:256])
                    pieces = [probs[:, kcn * 128:(kcn + 1) * 128] for kcn in range(32)]

                    def evac(idx, pap, tgt, qt=qt):
                        P.step("vector",
                               lambda eng, idx=idx, pap=pap, qt=qt: eng.tensor_copy(
                                   out=probsT[:, qt * S + idx * 128: qt * S + (idx + 1) * 128],
                                   in_=pap),
                               waits=[("t", tgt)], incs=[("v", 1)])

                    transpose_seq(P, pieces, [tp_a, tp_b], sident,
                                  pre_waits=[("v", P.planned("v")),
                                             ("s", P.planned("s"))], evac=evac)
                if stop == "d_pT" and (b, kv, qh) == (0, 0, 0):
                    _dump(P, dbg, probsT[:], 4 * S, x_sb, x1); return nc
                if stop == "d_pTs" and (b, kv, qh) == (0, 0, 0):
                    P.dma("sync", dbgh[:, 0:S], probsT[:, 0:S],
                          waits=[("v", P.planned("v")), ("s", P.planned("s")),
                                 ("t", P.planned("t")), ("dq_scalar", P.qcount("scalar"))])
                    P.dma("sync", dbgh[:, S:2 * S], probsT[:, 0:S])
                    P.dma("sync", dbgh[:, 2 * S:3 * S], probsT[:, S:2 * S])
                    P.step("sync", None, waits=[("dq_sync", P.qcount("sync"))])
                    _cut(P, x_sb, x1, extra_waits=[("dq_sync", P.qcount("sync"))]); return nc
                if stop == "d_pTh" and (b, kv, qh) == (0, 0, 0):
                    P.dma("scalar", dbgh[:], probsT[:],
                          waits=[("v", P.planned("v")), ("s", P.planned("s")),
                                 ("t", P.planned("t")), ("dq_sync", P.qcount("sync"))])
                    P.step("scalar", None, waits=[("dq_scalar", P.qcount("scalar"))])
                    _cut(P, x_sb, x1, extra_waits=[("dq_scalar", P.qcount("scalar"))]); return nc
                if stop == "d_p2x" and (b, kv, qh) == (0, 0, 0):
                    P.dma("scalar", dbgh[:, 2 * S:3 * S], probsT[:, 0:S],
                          waits=[("v", P.planned("v")), ("s", P.planned("s")),
                                 ("t", P.planned("t"))])
                    P.step("scalar", None, waits=[("dq_scalar", P.qcount("scalar"))])
                    _cut(P, x_sb, x1); return nc
                probsT4 = probsT[:].rearrange("p (qt k) -> p qt k", qt=4)
                P.step("tensor", None, waits=[("v", P.planned("v"))])
                for kcn in range(32):
                    P.step("tensor",
                           lambda eng, kcn=kcn, probsT4=probsT4: eng.matmul(
                               out=ctx_ps[:],
                               lhsT=vtsb[:, kcn * 128:(kcn + 1) * 128],
                               rhs=probsT4[:, :, kcn * 128:(kcn + 1) * 128],
                               start=(kcn == 0), stop=(kcn == 31)),
                           incs=([("t", 1)] if kcn == 31 else []))
                P.step("vector",
                       lambda eng, h=h: eng.tensor_copy(
                           out=ctx[:, h * TB:(h + 1) * TB], in_=ctx_ps[:]),
                       waits=[("t", P.planned("t"))], incs=[("v", 1)])

        if stop == "d_ctx" and b == 0:
            _dump(P, dbg, ctx[:], NH * TB, x_sb, x1); return nc
        ctx_ready = P.planned("v")
        for ot in range(KC):
            wb, wready = load_wblock(wo, ot)
            P.step("tensor", None,
                   waits=[("g", wready), ("v", ctx_ready),
                          ("v", P.planned("v"))])
            for h in range(NH):
                P.step("tensor",
                       lambda eng, h=h, wb=wb: eng.matmul(
                           out=pj_ps[:],
                           lhsT=wb[:, h * 128:(h + 1) * 128],
                           rhs=ctx[:, h * TB:(h + 1) * TB],
                           start=(h == 0), stop=(h == NH - 1)),
                       incs=([("t", 1)] if h == NH - 1 else []))
            P.step("vector",
                   lambda eng: eng.tensor_copy(out=ot_f[:], in_=pj_ps[:]),
                   waits=[("t", P.planned("t"))], incs=[("v", 1)])
            if stop == "d_of" and (b, ot) == (0, 0):
                _dump(P, dbg, ot_f[:], 512, x_sb, x1); return nc
            pieces = [ot_f[:, j * 128:(j + 1) * 128] for j in range(4)]

            def evac(idx, pap, tgt):
                P.step("vector",
                       lambda eng, idx=idx, pap=pap: eng.tensor_copy(
                           out=ot_t[:, idx * 128:(idx + 1) * 128], in_=pap),
                       waits=[("t", tgt)], incs=[("v", 1)])

            transpose_seq(P, pieces, [tf_a, tf_b], sidentf,
                          pre_waits=[("v", P.planned("v"))], evac=evac)
            for j in range(4):
                ti = b * 4 + j
                P.step("vector",
                       lambda eng, ti=ti, j=j, ot=ot: eng.tensor_tensor(
                           out=x_sb[ti][:, ot * 128:(ot + 1) * 128],
                           in0=x_sb[ti][:, ot * 128:(ot + 1) * 128],
                           in1=ot_t[:, j * 128:(j + 1) * 128],
                           op=ALU.add),
                       incs=([("v", 1)] if j == 3 else []))

    if stop == "attn":
        _cut(P, x_sb, x1); return nc

    if stop == "d_res":
        _dump(P, dbg, x_sb[0][:], D, x_sb, x1); return nc

    ln_tag = emit_layernorm_inplace(P, x_sb, (ln_nsum, ln_rstd, ln_xc), "ln0a")
    ln_done = P.planned(ln_tag)
    for i in range(NT):
        P.dma("sync", x1[i * 128:(i + 1) * 128, :], x_sb[i][:],
              waits=[(ln_tag, ln_done)] if i == 0 else [])
    P.step("sync", None, waits=[("dq_sync", P.qcount("sync"))])

    P.emit()
    return nc



def build_l0_ffn(stop=None):
    nc = bass.Bass("TRN2", target_bir_lowering=False, debug=False)
    NKVH = 16

    hs = nc.dram_tensor("hs", [T, D], F32, kind="ExternalInput").ap()
    w_router = nc.dram_tensor("w_router", [128, KC * 128], F32, kind="ExternalInput").ap()
    w_down = nc.dram_tensor("w_down", [128, KC * 128], F32, kind="ExternalInput").ap()
    w_up = nc.dram_tensor("w_up", [D, 128], F32, kind="ExternalInput").ap()
    sbias = nc.dram_tensor("sbias", [1, D], F32, kind="ExternalInput").ap()
    g1 = nc.dram_tensor("g1", [1, D], F32, kind="ExternalInput").ap()
    b1 = nc.dram_tensor("b1", [1, D], F32, kind="ExternalInput").ap()
    g2 = nc.dram_tensor("g2", [1, D], F32, kind="ExternalInput").ap()
    b2 = nc.dram_tensor("b2", [1, D], F32, kind="ExternalInput").ap()
    wq = nc.dram_tensor("wq", [D, D], F32, kind="ExternalInput").ap()
    wk = nc.dram_tensor("wk", [D, D], F32, kind="ExternalInput").ap()
    wv = nc.dram_tensor("wv", [D, D], F32, kind="ExternalInput").ap()
    wo = nc.dram_tensor("wo", [D, D], F32, kind="ExternalInput").ap()
    wf1 = nc.dram_tensor("wf1", [4 * D, D], F32, kind="ExternalInput").ap()
    wf2 = nc.dram_tensor("wf2", [D, 4 * D], F32, kind="ExternalInput").ap()
    cmask = nc.dram_tensor("cmask", [4 * 128, S], BF16, kind="ExternalInput").ap()
    onesf = nc.dram_tensor("onesf", [1, 128], F32, kind="ExternalInput").ap()
    ident = nc.dram_tensor("ident", [128, 128], BF16, kind="ExternalInput").ap()
    identf = nc.dram_tensor("identf", [128, 128], F32, kind="ExternalInput").ap()
    h2 = nc.dram_tensor("h2out", [T, D], F32, kind="ExternalOutput").ap()
    dbg = nc.dram_tensor("dbg", [128, 16384], F32, kind="ExternalOutput").ap()

    kgin = nc.dram_tensor("kgin", [B * NKVH * 128, TB], BF16)
    kg = nc.dram_tensor("kgather", [NCORE * B * NKVH * 128, TB], BF16)
    vgin = nc.dram_tensor("vgin", [B * NKVH * TB, 128], BF16)
    vg = nc.dram_tensor("vgather", [NCORE * B * NKVH * TB, 128], BF16)

    P = Core(nc)

    o = [16640]
    def take(n):
        off = o[0]; o[0] += (n + 63) // 64 * 64
        return off
    h_sb = [P.sba(f"ht{i}", [128, D], F32, take(D * 4)) for i in range(NT)]
    sident = P.sba("sident", [128, 128], BF16, take(256))
    sidentf = P.sba("sidentf", [128, 128], F32, take(512))
    sonesf = P.sba("sonesf", [1, 128], F32, take(512))
    gb = {"sbb": P.sba("sbb", [128, D], F32, take(D * 4))}
    gbf = {nm: P.sba(nm, [128, KC], F32, take(KC * 4))
           for nm in ("g1f", "b1f", "g2f", "b2f")}
    midbf = P.sba("midbf", [128, T], BF16, take(T * 2))
    masks = P.sba("masksb", [128, 32], F32, take(128))
    wstage = P.sba("wstage", [128, KC * 128], F32, take(KC * 128 * 4))
    wbuf = [P.sba(f"wbuf{i}", [128, KC * 128], BF16, take(KC * 128 * 2)) for i in range(2)]
    nsum = P.sba("nsum", [128, 1], F32, take(64))
    rstd = P.sba("rstd", [128, 1], F32, take(64))
    nmax = P.sba("nmax", [128, 1], F32, take(64))
    ssum = P.sba("ssum", [128, 1], F32, take(64))
    rsum = P.sba("rsum", [128, 1], F32, take(64))
    rsum2 = P.sba("rsum2", [128, 1], F32, take(64))
    flush = P.sba("flush", [128, 256], BF16, take(512))
    _SPACER_TILE[0] = flush[:, 128:192]
    A = take(0)
    wrt = P.sba("wrt", [128, KC * 128], F32, A)
    brow_st = P.sba("brow_st", [1, D], F32, A + 8192)
    xn1_all = P.sba("xn1_all", [128, KC * T], BF16, A)
    kv_sc = P.sba("kv_sc", [128, D], F32, A + 32768)
    kv_bf = P.sba("kv_bf", [128, D], BF16, A + 40960)
    kv_out = P.sba("kv_out", [128, T], F32, A + 45056)
    kv_ob = P.sba("kv_ob", [128, T], BF16, A + 49152)
    vt_bf2 = P.sba("vt_bf2", [128, 1024], BF16, A + 51200)
    hftile = P.sba("hftile", [128, D], F32, A + 53248)
    hfbf = P.sba("hfbf", [128, D], BF16, A + 61440)
    logitsF = P.sba("logitsF", [128, T], F32, A + 65536)
    logitsT = P.sba("logitsT", [128, 8 * 128], F32, A + 69632)
    xn1g = P.sba("xn1g", [128, KC * 256], BF16, A)
    qg = P.sba("qg", [128, NH * 256], BF16, A + 8192)
    ksb = P.sba("ksb2", [128, S], BF16, A + 16384)
    vtsb = P.sba("vtsb2", [128, S], BF16, A + 24576)
    scores = P.sba("scores2", [128, S], F32, A + 32768)
    probsT = P.sba("probsT2", [128, S], BF16, A + 32768)
    probs = P.sba("probs2", [128, S], BF16, A + 49152)
    maskg = P.sba("maskg", [128, 2 * S], BF16, A + 57344)
    ctxg = P.sba("ctxg", [128, NH * 256], BF16, A + 73728)
    x1g = [P.sba(f"x1g{i}", [128, D], F32, A + 81920 + i * 8192) for i in range(2)]
    gel = [P.sba(f"gel{i}", [128, 256], F32, A + 98304 + i * 1024) for i in range(2)]
    otf = P.sba("otf", [128, 256], F32, A + 100352)
    ott = P.sba("ott", [128, 256], F32, A + 101376)
    h1g = P.sba("h1g", [128, 64 * 256], BF16, A)
    xn2g = P.sba("xn2g", [128, KC * 256], BF16, A + 32768)
    wstage2 = P.sba("wstage2", [128, 4096], F32, A + 40960)
    wbuf2 = P.sba("wbuf2", [128, 8192], BF16, A + 57344)
    selt = P.sba("selt", [128, D], F32, A + 49152)
    lnt = P.sba("lnt", [128, D], F32, A + 57344)
    dltg = P.sba("dltg", [128, 2 * D], BF16, A + 73728)
    lnsc = P.sba("lnsc", [128, D], F32, A + 102400)
    lnbf = P.sba("lnbf", [128, D], BF16, A + 110592)
    assert A + 114688 <= 229376, f"sbuf overflow {A + 114688}"

    tp_a = P.ps("tp_a", [128, 1024], BF16)
    tp_b = P.ps("tp_b", [128, 1024], BF16)
    tf_a = P.ps("tf_a", [128, 512], F32)
    tf_b = P.ps("tf_b", [128, 512], F32)
    sc_ps = P.ps("sc_ps", [128, 1024], F32)
    ctx_ps = P.ps("ctx_ps", [128, 512], F32)
    pj_ps = P.ps("pj_ps", [128, 512], F32)

    for snm in ("t", "v", "s", "cc", "g"):
        P.sem(snm)

    P.step("scalar", lambda eng: eng.memzero(nmax[:]))
    P.step("scalar", lambda eng: eng.activation(out=nmax[:], in_=nmax[:], func=AF.Square))
    P.step("scalar", lambda eng: eng.activation(out=nmax[:], in_=nmax[:], func=AF.Exp))
    P.step("scalar", lambda eng: eng.activation(out=nmax[:], in_=nmax[:], func=AF.Ln))
    P.step("scalar", lambda eng: eng.activation(out=nmax[:], in_=nmax[:], func=AF.Tanh))
    P.step("scalar", lambda eng: eng.memzero(nmax[:]), incs=[("s", 1)])

    P.dma("sync", sident[:], ident)
    P.dma("sync", sidentf[:], identf)
    P.dma("sync", sonesf[:], onesf)
    for i in range(NT):
        P.dma("sync", h_sb[i][:], hs[i * 128:(i + 1) * 128, :])
    loads_done = P.qcount("sync")

    for nm, ap in (("sbb", sbias),):
        P.dma("sync", brow_st[:], ap, waits=[("t", P.planned("t"))])
        ldc = P.qcount("sync")
        for c in range(4):
            P.step("tensor",
                   lambda eng, c=c: eng.matmul(
                       out=pj_ps[:], lhsT=sonesf[:],
                       rhs=brow_st[:, c * 512:(c + 1) * 512],
                       start=True, stop=True),
                   waits=[("dq_sync", ldc), ("v", P.planned("v"))],
                   incs=[("t", 1)])
            P.step("vector",
                   lambda eng, nm=nm, c=c: eng.tensor_copy(
                       out=gb[nm][:, c * 512:(c + 1) * 512], in_=pj_ps[:]),
                   waits=[("t", P.planned("t"))], incs=[("v", 1)])
    for nm, ap in (("g1f", g1), ("b1f", b1), ("g2f", g2), ("b2f", b2)):
        P.dma("sync", gbf[nm][:],
              ap.rearrange("a (kc p) -> (a p) kc", p=128),
              allow_slow_non_contiguous=True)

    def load_wblock(wsrc, ot, stage=None, buf=None, width=KC * 128):
        stage = stage if stage is not None else wstage
        wb = buf if buf is not None else wbuf[ot % 2]
        P.dma("sync", stage[:, 0:width], wsrc[ot * 128:(ot + 1) * 128, 0:width],
              waits=[("g", P.planned("g"))])
        P.step("gpsimd",
               lambda eng, wb=wb: eng.tensor_copy(out=wb[:, 0:width], in_=stage[:, 0:width]),
               waits=[("dq_sync", P.qcount("sync")), ("t", P.planned("t"))],
               incs=[("g", 1)])
        return wb, P.planned("g")

    def ln_stats(xt):
        P.step("vector",
               lambda eng: eng.reduce_sum(out=nsum[:], in_=xt[:], axis=AX.X,
                                          negate=True),
               waits=[("s", P.planned("s")), ("v", P.planned("v"))])
        P.step("vector", lambda eng: eng.tensor_copy(out=flush[:, 192:256], in_=flush[:, 192:256]))
        P.step("vector",
               lambda eng: eng.tensor_scalar(out=nsum[:], in0=nsum[:],
                                             scalar1=1.0 / D, scalar2=None, op0=ALU.mult))
        P.step("vector", lambda eng: eng.tensor_copy(out=flush[:, 192:256], in_=flush[:, 192:256]))
        P.step("vector",
               lambda eng, xt=xt: eng.tensor_tensor(
                   out=lnsc[:], in0=xt[:], in1=nsum[:, 0:1].broadcast_to([128, D]),
                   op=ALU.add),
               incs=[("v", 1)])
        P.step("scalar",
               lambda eng: eng.activation(out=lnsc[:], in_=lnsc[:], func=AF.Square,
                                          accum_out=rstd[:]),
               waits=[("v", P.planned("v"))])
        P.step("scalar", lambda eng: eng_spacer(eng), incs=[("s", 1)])
        P.step("vector",
               lambda eng: eng.tensor_scalar(out=rstd[:], in0=rstd[:], scalar1=1.0 / D,
                                             scalar2=1e-5, op0=ALU.mult, op1=ALU.add),
               waits=[("s", P.planned("s"))], incs=[("v", 1)])
        P.step("scalar",
               lambda eng: eng.activation(out=rstd[:], in_=rstd[:], func=AF.Ln),
               waits=[("v", P.planned("v"))])
        P.step("scalar", lambda eng: eng_spacer(eng), incs=[("s", 1)])
        P.step("vector",
               lambda eng: eng.tensor_scalar(out=rstd[:], in0=rstd[:], scalar1=-0.5,
                                             scalar2=None, op0=ALU.mult),
               waits=[("s", P.planned("s"))], incs=[("v", 1)])
        P.step("scalar",
               lambda eng: eng.activation(out=rstd[:], in_=rstd[:], func=AF.Exp),
               waits=[("v", P.planned("v"))])
        P.step("scalar", lambda eng: eng_spacer(eng), incs=[("s", 1)])

    def ln_apply_fm(xt, gname, bname, dst_fm, ntok, col0):
        P.step("vector",
               lambda eng, xt=xt: eng.tensor_tensor(
                   out=lnsc[:], in0=xt[:], in1=nsum[:, 0:1].broadcast_to([128, D]),
                   op=ALU.add),
               waits=[("s", P.planned("s"))])
        P.step("vector",
               lambda eng: eng.tensor_tensor(
                   out=lnsc[:], in0=lnsc[:], in1=rstd[:, 0:1].broadcast_to([128, D]),
                   op=ALU.mult))
        P.step("vector",
               lambda eng: eng.tensor_copy(out=lnbf[:], in_=lnsc[:]),
               incs=[("v", 1)])
        pieces = [lnbf[:, kc * 128:(kc + 1) * 128] for kc in range(KC)]
        gf, bf = gbf[gname], gbf[bname]

        def evac(idx, pap, tgt):
            P.step("vector",
                   lambda eng, idx=idx, pap=pap: eng.scalar_tensor_tensor(
                       out=dst_fm[:, idx * ntok + col0: idx * ntok + col0 + 128],
                       in0=pap, scalar=gf[:, idx:idx + 1],
                       in1=bf[:, idx:idx + 1].broadcast_to([128, 128]),
                       op0=ALU.mult, op1=ALU.add),
                   waits=[("t", tgt)], incs=[("v", 1)])

        transpose_seq(P, pieces, [tp_a, tp_b], sident,
                      pre_waits=[("v", P.planned("v"))], evac=evac)

    for i in range(NT):
        pieces = [h_sb[i][:, kc * 128:(kc + 1) * 128] for kc in range(KC)]

        def evacf(idx, pap, tgt):
            P.step("vector",
                   lambda eng, idx=idx, pap=pap: eng.tensor_copy(
                       out=hftile[:, idx * 128:(idx + 1) * 128], in_=pap),
                   waits=[("t", tgt)], incs=[("v", 1)])

        if i == 0:
            P.step("vector", None, waits=[("dq_sync", loads_done)], incs=[("v", 1)])
        transpose_seq(P, pieces, [tf_a, tf_b], sidentf,
                      pre_waits=[("v", P.planned("v"))], evac=evacf)
        P.step("vector",
               lambda eng: eng.tensor_copy(out=hfbf[:], in_=hftile[:]),
               incs=[("v", 1)])
        rl_ld = P.qcount("sync")
        if i == 0:
            P.dma("sync", wrt[:], w_router)
        rl_ld = P.qcount("sync")
        for kc in range(KC):
            P.step("tensor",
                   lambda eng, kc=kc: eng.matmul(
                       out=pj_ps[:, 0:128],
                       lhsT=wrt[:, kc * 128:(kc + 1) * 128],
                       rhs=hftile[:, kc * 128:(kc + 1) * 128],
                       start=(kc == 0), stop=(kc == KC - 1)),
                   waits=([("dq_sync", rl_ld), ("v", P.planned("v"))] if kc == 0 else []),
                   incs=([("t", 1)] if kc == KC - 1 else []))
        P.step("vector",
               lambda eng, i=i: eng.tensor_copy(out=logitsF[:, i * 128:(i + 1) * 128],
                                                in_=pj_ps[:, 0:128]),
               waits=[("t", P.planned("t"))], incs=[("v", 1)])
        if i == 0:
            wbd, wdg = load_wblock(w_down, 0)
        for kc in range(KC):
            P.step("tensor",
                   lambda eng, kc=kc: eng.matmul(
                       out=ctx_ps[:, 0:128],
                       lhsT=wbuf[0][:, kc * 128:(kc + 1) * 128],
                       rhs=hfbf[:, kc * 128:(kc + 1) * 128],
                       start=(kc == 0), stop=(kc == KC - 1)),
                   waits=([("g", P.planned("g")), ("v", P.planned("v"))] if kc == 0 else []),
                   incs=([("t", 1)] if kc == KC - 1 else []))
        P.step("vector",
               lambda eng, i=i: eng.tensor_copy(out=midbf[:, i * 128:(i + 1) * 128],
                                                in_=ctx_ps[:, 0:128]),
               waits=[("t", P.planned("t"))], incs=[("v", 1)])

    pieces = [logitsF[:, i * 128:(i + 1) * 128] for i in range(NT)]

    def evacl(idx, pap, tgt):
        P.step("vector",
               lambda eng, idx=idx, pap=pap: eng.tensor_copy(
                   out=logitsT[:, idx * 128:(idx + 1) * 128], in_=pap),
               waits=[("t", tgt)], incs=[("v", 1)])

    transpose_seq(P, pieces, [tf_a, tf_b], sidentf,
                  pre_waits=[("v", P.planned("v"))], evac=evacl)
    for i in range(NT):
        la = logitsT[:, i * 128 + 0:i * 128 + 1]
        lb = logitsT[:, i * 128 + 1:i * 128 + 2]
        lc = logitsT[:, i * 128 + 2:i * 128 + 3]
        m0 = masks[:, i * 3 + 0:i * 3 + 1]
        m1 = masks[:, i * 3 + 1:i * 3 + 2]
        m2 = masks[:, i * 3 + 2:i * 3 + 3]
        s0 = masks[:, 24:25]; s1 = masks[:, 25:26]; s2 = masks[:, 26:27]
        s3 = masks[:, 27:28]; s4 = masks[:, 28:29]
        sp = lambda eng: eng.tensor_copy(out=flush[:, 192:256], in_=flush[:, 192:256])
        P.step("vector", lambda eng, a=lb, b=la, o=s0: eng.tensor_tensor(out=o, in0=a, in1=b, op=ALU.is_gt))
        P.step("vector", sp)
        P.step("vector", lambda eng, a=lc, b=la, o=s1: eng.tensor_tensor(out=o, in0=a, in1=b, op=ALU.is_gt))
        P.step("vector", sp)
        P.step("vector", lambda eng, a=lc, b=lb, o=s2: eng.tensor_tensor(out=o, in0=a, in1=b, op=ALU.is_gt))
        P.step("vector", sp)
        P.step("vector", lambda eng, o=m2, a=s1, b=s2: eng.tensor_tensor(out=o, in0=a, in1=b, op=ALU.mult))
        P.step("vector", sp)
        P.step("vector", lambda eng, o=s3, a=s2: eng.tensor_scalar(out=o, in0=a, scalar1=-1.0, scalar2=1.0, op0=ALU.mult, op1=ALU.add))
        P.step("vector", sp)
        P.step("vector", lambda eng, o=m1, a=s0, b=s3: eng.tensor_tensor(out=o, in0=a, in1=b, op=ALU.mult))
        P.step("vector", sp)
        P.step("vector", lambda eng, o=s4, a=m1, b=m2: eng.tensor_tensor(out=o, in0=a, in1=b, op=ALU.add))
        P.step("vector", sp)
        P.step("vector", lambda eng, o=m0, a=s4: eng.tensor_scalar(out=o, in0=a, scalar1=-1.0, scalar2=1.0, op0=ALU.mult, op1=ALU.add),
               incs=[("v", 1)])

    for i in range(NT):
        ln_stats(h_sb[i])
        ln_apply_fm(h_sb[i], "g1f", "b1f", xn1_all, T, i * 128)
    if stop == "f_xn1":
        _dump(P, dbg, xn1_all[:], KC * T, h_sb, h2)
        return nc
    for ot in range(NKVH):
        wb, wg = load_wblock(wk, ot)
        for b in range(B):
            for kc in range(KC):
                P.step("tensor",
                       lambda eng, kc=kc, b=b, wb=wb: eng.matmul(
                           out=pj_ps[:],
                           lhsT=wb[:, kc * 128:(kc + 1) * 128],
                           rhs=xn1_all[:, kc * T + b * TB: kc * T + (b + 1) * TB],
                           start=(kc == 0), stop=(kc == KC - 1)),
                       waits=([("g", wg), ("v", P.planned("v"))] if kc == 0 else []),
                       incs=([("t", 1)] if kc == KC - 1 else []))
            P.step("vector",
                   lambda eng, b=b: eng.tensor_copy(out=kv_ob[:, b * TB:(b + 1) * TB],
                                                    in_=pj_ps[:]),
                   waits=[("t", P.planned("t"))], incs=[("v", 1)])
        vdone = P.planned("v")
        for b in range(B):
            row0 = (b * NKVH + ot) * 128
            P.dma("sync", kgin.ap()[row0:row0 + 128, :], kv_ob[:, b * TB:(b + 1) * TB],
                  waits=[("v", vdone)] if b == 0 else [])
    for ot in range(NKVH):
        wb, wg = load_wblock(wv, ot)
        for b in range(B):
            for kc in range(KC):
                P.step("tensor",
                       lambda eng, kc=kc, b=b, wb=wb: eng.matmul(
                           out=pj_ps[:],
                           lhsT=wb[:, kc * 128:(kc + 1) * 128],
                           rhs=xn1_all[:, kc * T + b * TB: kc * T + (b + 1) * TB],
                           start=(kc == 0), stop=(kc == KC - 1)),
                       waits=([("g", wg), ("v", P.planned("v"))] if kc == 0 else []),
                       incs=([("t", 1)] if kc == KC - 1 else []))
            P.step("vector",
                   lambda eng, b=b: eng.tensor_copy(out=kv_ob[:, b * TB:(b + 1) * TB],
                                                    in_=pj_ps[:]),
                   waits=[("t", P.planned("t"))], incs=[("v", 1)])
        pieces = [kv_ob[:, tt * 128:(tt + 1) * 128] for tt in range(8)]

        def evacv(idx, pap, tgt, ot=ot):
            P.step("vector",
                   lambda eng, idx=idx, pap=pap: eng.tensor_copy(
                       out=vt_bf2[:, idx * 128:(idx + 1) * 128], in_=pap),
                   waits=[("t", tgt), ("dq_sync", P.qcount("sync"))],
                   incs=[("v", 1)])
            vdone = P.planned("v")
            b, trow = (0, idx) if idx < 4 else (1, idx - 4)
            row0 = (b * NKVH + ot) * TB + trow * 128
            P.dma("sync", vgin.ap()[row0:row0 + 128, :],
                  vt_bf2[:, idx * 128:(idx + 1) * 128],
                  waits=[("v", vdone)])

        transpose_seq(P, pieces, [tp_a, tp_b], sident,
                      pre_waits=[("v", P.planned("v"))], evac=evacv)

    P.step("gpsimd",
           lambda eng: eng.collective_compute(
               "AllGather", ALU.bypass, replica_groups=[list(range(NCORE))],
               ins=[kgin.ap()], outs=[kg.ap()]),
           waits=[("dq_sync", P.qcount("sync"))], incs=[("cc", 1)])
    P.step("gpsimd",
           lambda eng: eng.collective_compute(
               "AllGather", ALU.bypass, replica_groups=[list(range(NCORE))],
               ins=[vgin.ap()], outs=[vg.ap()]),
           incs=[("cc", 1)])
    cc_done = P.planned("cc")

    GT = 256
    for g in range(4):
        b, half = g // 2, g % 2
        tiles = [b * 4 + half * 2, b * 4 + half * 2 + 1]
        for tin, ti in enumerate(tiles):
            ln_stats(h_sb[ti])
            ln_apply_fm(h_sb[ti], "g1f", "b1f", xn1g, GT, tin * 128)
        for ot in range(NH):
            wb, wg = load_wblock(wq, ot)
            for kc in range(KC):
                P.step("tensor",
                       lambda eng, kc=kc, wb=wb: eng.matmul(
                           out=pj_ps[:, 0:GT],
                           lhsT=wb[:, kc * 128:(kc + 1) * 128],
                           rhs=xn1g[:, kc * GT:(kc + 1) * GT],
                           start=(kc == 0), stop=(kc == KC - 1)),
                       waits=([("g", wg), ("v", P.planned("v"))] if kc == 0 else []),
                       incs=([("t", 1)] if kc == KC - 1 else []))
            P.step("vector",
                   lambda eng, ot=ot: eng.tensor_scalar(
                       out=qg[:, ot * GT:(ot + 1) * GT], in0=pj_ps[:, 0:GT],
                       scalar1=INVSQ, scalar2=None, op0=ALU.mult),
                   waits=[("t", P.planned("t"))], incs=[("v", 1)])
        P.dma("sync", maskg[:, 0:S], cmask[(half * 2) * 128:(half * 2 + 1) * 128, :],
              waits=[("s", P.planned("s")), ("v", P.planned("v"))])
        P.dma("sync", maskg[:, S:2 * S], cmask[(half * 2 + 1) * 128:(half * 2 + 2) * 128, :])
        mask_ld = P.qcount("sync")
        for tin, ti in enumerate(tiles):
            P.step("vector",
                   lambda eng, tin=tin, ti=ti: eng.tensor_copy(out=x1g[tin][:], in_=h_sb[ti][:]),
                   incs=[("v", 1)])
        for hh in range(NKVH):
            kbase = (b * NKVH + hh) * 128 * TB
            kap = bass.AP(kg, kbase, [[TB, 128], [B * NKVH * 128 * TB, NCORE], [1, TB]])
            P.dma("sync", ksb[:].rearrange("p (r t) -> p r t", r=NCORE), kap,
                  waits=[("cc", cc_done), ("t", P.planned("t"))])
            vbase = (b * NKVH + hh) * TB * 128
            vap = bass.AP(vg, vbase, [[128, 128], [B * NKVH * TB * 128, NCORE],
                                      [128 * 128, 4], [1, 128]])
            for r in range(NCORE):
                P.dma("sync",
                      vtsb[:, r * 512:(r + 1) * 512].rearrange("p (tb d) -> p tb d", tb=4),
                      bass.AP(vg, vbase + r * B * NKVH * TB * 128,
                              [[128, 128], [128 * 128, 4], [1, 128]]))
            kv_ld = P.qcount("sync")
            for qt2 in range(2):
                qoff = hh * GT + qt2 * 128
                for rnd in range(4):
                    P.step("tensor", None,
                           waits=[("dq_sync", max(kv_ld, mask_ld)), ("v", P.planned("v")),
                                  ("s", P.planned("s"))])
                    for c in range(2):
                        ch = rnd * 2 + c
                        P.step("tensor",
                               lambda eng, ch=ch, c=c, qoff=qoff: eng.matmul(
                                   out=sc_ps[:, c * 512:(c + 1) * 512],
                                   lhsT=qg[:, qoff:qoff + 128],
                                   rhs=ksb[:, ch * 512:(ch + 1) * 512],
                                   start=True, stop=True),
                               incs=([("t", 1)] if c == 1 else []))
                    P.step("vector",
                           lambda eng, rnd=rnd, qt2=qt2: eng.scalar_tensor_tensor(
                               out=scores[:, rnd * 1024:(rnd + 1) * 1024],
                               in0=sc_ps[:], scalar=1.0,
                               in1=maskg[:, qt2 * S + rnd * 1024: qt2 * S + (rnd + 1) * 1024],
                               op0=ALU.mult, op1=ALU.add),
                           waits=[("t", P.planned("t"))], incs=[("v", 1)])
                P.step("vector",
                       lambda eng: eng.reduce_max(out=nmax[:], in_=scores[:],
                                                  axis=AX.X, negate=True))
                P.step("vector",
                       lambda eng: eng.tensor_copy(out=rsum[:], in_=nmax[:]),
                       incs=[("v", 1)])
                P.step("scalar",
                       lambda eng: eng.activation(out=probs[:], in_=scores[:],
                                                  func=AF.Exp, bias=rsum[:],
                                                  accum_out=ssum[:]),
                       waits=[("v", P.planned("v"))])
                P.step("scalar", lambda eng: eng_spacer(eng), incs=[("s", 1)])
                P.step("vector",
                       lambda eng: eng.reciprocal(out=rsum2[:], in_=ssum[:]),
                       waits=[("s", P.planned("s"))])
                P.step("vector",
                       lambda eng: eng.tensor_copy(out=flush[:, 192:256], in_=flush[:, 192:256]))
                P.step("vector",
                       lambda eng: eng.tensor_tensor(
                           out=probs[:], in0=probs[:],
                           in1=rsum2[:, 0:1].broadcast_to([128, S]), op=ALU.mult),
                       incs=[("v", 1)])
                pieces = [probs[:, kcn * 128:(kcn + 1) * 128] for kcn in range(32)]

                def evacp(idx, pap, tgt):
                    P.step("vector",
                           lambda eng, idx=idx, pap=pap: eng.tensor_copy(
                               out=probsT[:, idx * 128:(idx + 1) * 128], in_=pap),
                           waits=[("t", tgt)], incs=[("v", 1)])

                transpose_seq(P, pieces, [tp_a, tp_b], sident,
                              pre_waits=[("v", P.planned("v")), ("s", P.planned("s"))],
                              evac=evacp)
                P.step("tensor", None, waits=[("v", P.planned("v"))])
                for kcn in range(32):
                    P.step("tensor",
                           lambda eng, kcn=kcn: eng.matmul(
                               out=ctx_ps[:, 0:128],
                               lhsT=vtsb[:, kcn * 128:(kcn + 1) * 128],
                               rhs=probsT[:, kcn * 128:(kcn + 1) * 128],
                               start=(kcn == 0), stop=(kcn == 31)),
                           incs=([("t", 1)] if kcn == 31 else []))
                P.step("vector",
                       lambda eng, hh=hh, qt2=qt2: eng.tensor_copy(
                           out=ctxg[:, hh * GT + qt2 * 128: hh * GT + (qt2 + 1) * 128],
                           in_=ctx_ps[:, 0:128]),
                       waits=[("t", P.planned("t"))], incs=[("v", 1)])
        if stop == "f_ctx" and g == 0:
            _dump(P, dbg, ctxg[:], NH * 256, h_sb, h2)
            return nc
        for ot in range(KC):
            wb, wg = load_wblock(wo, ot)
            P.step("tensor", None,
                   waits=[("g", wg), ("v", P.planned("v"))])
            for hh in range(NH):
                P.step("tensor",
                       lambda eng, hh=hh, wb=wb: eng.matmul(
                           out=pj_ps[:, 0:GT],
                           lhsT=wb[:, hh * 128:(hh + 1) * 128],
                           rhs=ctxg[:, hh * GT:(hh + 1) * GT],
                           start=(hh == 0), stop=(hh == NH - 1)),
                       incs=([("t", 1)] if hh == NH - 1 else []))
            P.step("vector",
                   lambda eng: eng.tensor_copy(out=otf[:], in_=pj_ps[:, 0:GT]),
                   waits=[("t", P.planned("t"))], incs=[("v", 1)])
            pieces = [otf[:, j * 128:(j + 1) * 128] for j in range(2)]

            def evaco(idx, pap, tgt, ot=ot):
                P.step("vector",
                       lambda eng, idx=idx, pap=pap, ot=ot: eng.tensor_tensor(
                           out=x1g[idx][:, ot * 128:(ot + 1) * 128],
                           in0=x1g[idx][:, ot * 128:(ot + 1) * 128],
                           in1=pap, op=ALU.add),
                       waits=[("t", tgt)], incs=[("v", 1)])

            transpose_seq(P, pieces, [tf_a, tf_b], sidentf,
                          pre_waits=[("v", P.planned("v"))], evac=evaco)
        if stop == "f_x1" and g == 0:
            _dump(P, dbg, x1g[0][:], D, h_sb, h2)
            return nc
        for tin in range(2):
            ln_stats(x1g[tin])
            ln_apply_fm(x1g[tin], "g2f", "b2f", xn2g, GT, tin * 128)
        for fo in range(64):
            wb, wg = load_wblock(wf1, fo)
            for kc in range(KC):
                P.step("tensor",
                       lambda eng, kc=kc, wb=wb: eng.matmul(
                           out=pj_ps[:, 0:GT],
                           lhsT=wb[:, kc * 128:(kc + 1) * 128],
                           rhs=xn2g[:, kc * GT:(kc + 1) * GT],
                           start=(kc == 0), stop=(kc == KC - 1)),
                       waits=([("g", wg), ("v", P.planned("v"))] if kc == 0 else []),
                       incs=([("t", 1)] if kc == KC - 1 else []))
            P.step("vector",
                   lambda eng: eng.tensor_copy(out=gel[0][:], in_=pj_ps[:, 0:GT]),
                   waits=[("t", P.planned("t"))], incs=[("v", 1)])
            P.step("scalar",
                   lambda eng: eng.activation(out=gel[1][:], in_=gel[0][:], func=AF.Square),
                   waits=[("v", P.planned("v"))])
            P.step("scalar", lambda eng: eng_spacer(eng), incs=[("s", 1)])
            P.step("vector",
                   lambda eng: eng.tensor_tensor(out=gel[1][:], in0=gel[1][:], in1=gel[0][:],
                                                 op=ALU.mult),
                   waits=[("s", P.planned("s"))])
            P.step("vector",
                   lambda eng: eng.tensor_scalar(out=gel[1][:], in0=gel[1][:],
                                                 scalar1=0.044715, scalar2=None, op0=ALU.mult))
            P.step("vector",
                   lambda eng: eng.tensor_tensor(out=gel[1][:], in0=gel[1][:], in1=gel[0][:],
                                                 op=ALU.add))
            P.step("vector",
                   lambda eng: eng.tensor_scalar(out=gel[1][:], in0=gel[1][:],
                                                 scalar1=0.7978845608028654, scalar2=None,
                                                 op0=ALU.mult),
                   incs=[("v", 1)])
            P.step("scalar",
                   lambda eng: eng.activation(out=gel[1][:], in_=gel[1][:], func=AF.Tanh),
                   waits=[("v", P.planned("v"))])
            P.step("scalar", lambda eng: eng_spacer(eng), incs=[("s", 1)])
            P.step("vector",
                   lambda eng: eng.tensor_scalar(out=gel[1][:], in0=gel[1][:],
                                                 scalar1=0.5, scalar2=0.5,
                                                 op0=ALU.mult, op1=ALU.add),
                   waits=[("s", P.planned("s"))])
            P.step("vector",
                   lambda eng: eng.tensor_tensor(out=gel[1][:], in0=gel[1][:], in1=gel[0][:],
                                                 op=ALU.mult))
            P.step("vector",
                   lambda eng, fo=fo: eng.tensor_copy(
                       out=h1g[:, fo * GT:(fo + 1) * GT], in_=gel[1][:]),
                   incs=[("v", 1)])
        for ot in range(KC):
            P.dma("sync", wstage2[:], wf2[ot * 128:(ot + 1) * 128, 0:4096],
                  waits=[("g", P.planned("g"))])
            P.step("gpsimd",
                   lambda eng: eng.tensor_copy(out=wbuf2[:, 0:4096], in_=wstage2[:]),
                   waits=[("dq_sync", P.qcount("sync")), ("t", P.planned("t"))],
                   incs=[("g", 1)])
            P.dma("sync", wstage2[:], wf2[ot * 128:(ot + 1) * 128, 4096:8192],
                  waits=[("g", P.planned("g"))])
            P.step("gpsimd",
                   lambda eng: eng.tensor_copy(out=wbuf2[:, 4096:8192], in_=wstage2[:]),
                   waits=[("dq_sync", P.qcount("sync"))],
                   incs=[("g", 1)])
            wg2 = P.planned("g")
            for fc in range(64):
                P.step("tensor",
                       lambda eng, fc=fc: eng.matmul(
                           out=pj_ps[:, 0:GT],
                           lhsT=wbuf2[:, fc * 128:(fc + 1) * 128],
                           rhs=h1g[:, fc * GT:(fc + 1) * GT],
                           start=(fc == 0), stop=(fc == 63)),
                       waits=([("g", wg2), ("v", P.planned("v"))] if fc == 0 else []),
                       incs=([("t", 1)] if fc == 63 else []))
            P.step("vector",
                   lambda eng: eng.tensor_copy(out=otf[:], in_=pj_ps[:, 0:GT]),
                   waits=[("t", P.planned("t"))], incs=[("v", 1)])
            pieces = [otf[:, j * 128:(j + 1) * 128] for j in range(2)]

            def evacs(idx, pap, tgt, ot=ot):
                P.step("vector",
                       lambda eng, idx=idx, pap=pap, ot=ot: eng.tensor_tensor(
                           out=x1g[idx][:, ot * 128:(ot + 1) * 128],
                           in0=x1g[idx][:, ot * 128:(ot + 1) * 128],
                           in1=pap, op=ALU.add),
                       waits=[("t", tgt)], incs=[("v", 1)])

            transpose_seq(P, pieces, [tf_a, tf_b], sidentf,
                          pre_waits=[("v", P.planned("v"))], evac=evacs)
        if stop == "f_out2" and g == 0:
            _dump(P, dbg, x1g[0][:], D, h_sb, h2)
            return nc
        for ot in range(KC):
            wb, wg = load_wblock(w_up, ot, width=128)
            P.step("tensor",
                   lambda eng, wb=wb, g=g: eng.matmul(
                       out=pj_ps[:, 0:GT], lhsT=wb[:, 0:128],
                       rhs=midbf[:, g * GT:(g + 1) * GT],
                       start=True, stop=True),
                   waits=[("g", wg), ("v", P.planned("v"))],
                   incs=[("t", 1)])
            P.step("vector",
                   lambda eng: eng.tensor_copy(out=lnbf[:, 0:GT], in_=pj_ps[:, 0:GT]),
                   waits=[("t", P.planned("t"))], incs=[("v", 1)])
            pieces = [lnbf[:, j * 128:(j + 1) * 128] for j in range(2)]

            def evacd(idx, pap, tgt, ot=ot):
                P.step("vector",
                       lambda eng, idx=idx, pap=pap, ot=ot: eng.tensor_copy(
                           out=dltg[:, idx * D + ot * 128: idx * D + (ot + 1) * 128],
                           in_=pap),
                       waits=[("t", tgt)], incs=[("v", 1)])

            transpose_seq(P, pieces, [tp_a, tp_b], sident,
                          pre_waits=[("v", P.planned("v"))], evac=evacd)
        if stop == "f_x1sel" and g == 0:
            _dump(P, dbg, x1g[0][:], D, h_sb, h2)
            return nc
        if stop == "f_dlt" and g == 0:
            _dump(P, dbg, dltg[:], 2 * D, h_sb, h2)
            return nc
        for tin, ti in enumerate(tiles):
            m0 = masks[:, ti * 3 + 0:ti * 3 + 1]
            m1 = masks[:, ti * 3 + 1:ti * 3 + 2]
            m2 = masks[:, ti * 3 + 2:ti * 3 + 3]
            P.step("vector",
                   lambda eng, ti=ti: eng.tensor_tensor(out=selt[:], in0=h_sb[ti][:],
                                                        in1=gb["sbb"][:], op=ALU.add),
                   waits=[("v", P.planned("v"))])
            P.step("vector",
                   lambda eng, m0=m0: eng.tensor_tensor(
                       out=selt[:], in0=selt[:],
                       in1=m0[:, 0:1].broadcast_to([128, D]), op=ALU.mult))
            P.step("vector",
                   lambda eng, ti=ti, tin=tin: eng.tensor_tensor(
                       out=lnt[:], in0=h_sb[ti][:],
                       in1=dltg[:, tin * D:(tin + 1) * D], op=ALU.add))
            P.step("vector",
                   lambda eng, m1=m1: eng.tensor_tensor(
                       out=lnt[:], in0=lnt[:],
                       in1=m1[:, 0:1].broadcast_to([128, D]), op=ALU.mult))
            P.step("vector",
                   lambda eng: eng.tensor_tensor(out=selt[:], in0=selt[:], in1=lnt[:],
                                                 op=ALU.add))
            P.step("vector",
                   lambda eng, tin=tin, m2=m2: eng.tensor_tensor(
                       out=lnt[:], in0=x1g[tin][:],
                       in1=m2[:, 0:1].broadcast_to([128, D]), op=ALU.mult))
            P.step("vector",
                   lambda eng: eng.tensor_tensor(out=selt[:], in0=selt[:], in1=lnt[:],
                                                 op=ALU.add))
            P.step("vector",
                   lambda eng, ti=ti: eng.tensor_tensor(out=h_sb[ti][:], in0=h_sb[ti][:],
                                                        in1=selt[:], op=ALU.add),
                   incs=[("v", 1)])
        if stop == "f_y" and g == 0:
            _dump(P, dbg, h_sb[tiles[0]][:], D, h_sb, h2)
            return nc
        lntag = emit_layernorm_inplace(P, [h_sb[t] for t in tiles],
                                       (nsum, rstd, lnt), f"lnf{g}")
        lnd = P.planned(lntag)
        for tin, ti in enumerate(tiles):
            P.dma("sync", h2[ti * 128:(ti + 1) * 128, :], h_sb[ti][:],
                  waits=[(lntag, lnd)] if tin == 0 else [])

    P.step("sync", None, waits=[("dq_sync", P.qcount("sync"))])
    P.emit()
    return nc


def build_l1_attn(stop=None):
    nc = bass.Bass("TRN2", target_bir_lowering=False, debug=False)

    xs = nc.dram_tensor("xs", [T, D], F32, kind="ExternalInput").ap()
    wqkv = nc.dram_tensor("wqkv", [3 * D, KC * 128], F32, kind="ExternalInput").ap()
    wo = nc.dram_tensor("wo", [D, KC * 128], F32, kind="ExternalInput").ap()
    esel = nc.dram_tensor("esel", [8, 128], F32, kind="ExternalInput").ap()
    ident = nc.dram_tensor("ident", [128, 128], BF16, kind="ExternalInput").ap()
    identf = nc.dram_tensor("identf", [128, 128], F32, kind="ExternalInput").ap()
    x3 = nc.dram_tensor("x3out", [T, D], F32, kind="ExternalOutput").ap()
    dbg = nc.dram_tensor("dbg", [128, 16384], F32, kind="ExternalOutput").ap()
    dbgh = nc.dram_tensor("dbgh", [128, 16384], BF16, kind="ExternalOutput").ap()

    kgin = nc.dram_tensor("kgin", [B * NH * 128, TB], BF16)
    kg = nc.dram_tensor("kgather", [NCORE * B * NH * 128, TB], BF16)
    csin = nc.dram_tensor("csin", [NCORE, B * NH * 512], F32)
    csout = nc.dram_tensor("csout", [NCORE, B * NH * 512], F32)

    P = Core(nc)

    o = [16640]
    def take(n):
        off = o[0]; o[0] += (n + 63) // 64 * 64
        return off
    x_sb = [P.sba(f"xt{i}", [128, D], F32, take(D * 4)) for i in range(NT)]
    sident = P.sba("sident", [128, 128], BF16, take(256))
    sidentf = P.sba("sidentf", [128, 128], F32, take(512))
    sesel = P.sba("sesel", [8, 128], F32, take(512))
    wstage = P.sba("wstage", [128, KC * 128], F32, take(KC * 128 * 4))
    wbuf = [P.sba(f"wbuf{i}", [128, KC * 128], BF16, take(KC * 128 * 2)) for i in range(2)]
    ssum8 = P.sba("ssum8", [128, 32], F32, take(128))
    sA = P.sba("sA", [128, 1], F32, take(64))
    sB = P.sba("sB", [128, 1], F32, take(64))
    nsum = P.sba("nsum", [128, 1], F32, take(64))
    rstd = P.sba("rstd", [128, 1], F32, take(64))
    shift8 = P.sba("shift8", [128, 1], F32, take(64))
    rs4b = P.sba("rs4b", [128, 4], BF16, take(64))
    flush = P.sba("flush", [128, 256], BF16, take(512))
    _SPACER_TILE[0] = flush[:, 128:192]
    cst = [P.sba(f"cst{j}", [1, 512], F32, take(2048)) for j in range(2)]
    cs8 = P.sba("cs8", [8, 512], F32, take(2048))
    vloc = P.sba("vloc", [128, NH * T], BF16, take(NH * T * 2))
    qbf = P.sba("qbf", [128, NH * T], BF16, take(NH * T * 2))
    A = take(0)
    assert A + 40960 <= 229376, f"sbuf overflow {A}"
    xF = P.sba("xF", [128, KC * T], BF16, A)
    xbf = P.sba("xbf", [128, D], BF16, A + 32768)
    kst = [P.sba(f"kst{j}", [128, 512], BF16, A + 36864 + 1024 * j) for j in range(2)]
    ksb = P.sba("ksb", [128, S], BF16, A)
    probs = P.sba("probs", [128, 4 * S], BF16, A + 8192)
    csT = P.sba("csT", [128, 512], BF16, A)
    lnsc = P.sba("lnsc", [128, D], F32, A + 3072)
    vscl = P.sba("vscl", [128, NH * 512], BF16, A + 11264)
    pjf = P.sba("pjf", [128, 512], F32, A + 27648)
    pjt = P.sba("pjt", [128, 512], F32, A + 29696)

    tp_a = P.ps("tp_a", [128, 1024], BF16)
    tp_b = P.ps("tp_b", [128, 1024], BF16)
    tf_a = P.ps("tf_a", [128, 512], F32)
    tf_b = P.ps("tf_b", [128, 512], F32)
    sc = [P.ps("sc_a", [128, 512], F32), P.ps("sc_b", [128, 512], F32)]
    pj_ps = P.ps("pj_ps", [128, 512], F32)

    for snm in ("t", "v", "s", "cc", "g"):
        P.sem(snm)

    P.step("scalar", lambda eng: eng.memzero(sA[:]))
    P.step("scalar", lambda eng: eng.activation(out=sA[:], in_=sA[:], func=AF.Square))
    P.step("scalar", lambda eng: eng.activation(out=sA[:], in_=sA[:], func=AF.Exp))
    P.step("scalar", lambda eng: eng.activation(out=sA[:], in_=sA[:], func=AF.Ln))
    P.step("scalar", lambda eng: eng.memzero(sB[:]), incs=[("s", 1)])
    P.step("vector",
           lambda eng: eng.tensor_scalar(out=shift8[:], in0=sB[:],
                                         scalar1=-8.0, scalar2=None, op0=ALU.add),
           waits=[("s", P.planned("s"))])
    P.step("scalar", lambda eng: eng.memzero(sA[:]), incs=[("s", 1)])

    P.dma("sync", sident[:], ident)
    P.dma("sync", sidentf[:], identf)
    P.dma("sync", sesel[:], esel)
    for i in range(NT):
        P.dma("sync", x_sb[i][:], xs[i * 128:(i + 1) * 128, :])
    loads_done = P.qcount("sync")

    def load_wblock(wsrc, ot):
        wb = wbuf[ot % 2]
        P.dma("sync", wstage[:], wsrc[ot * 128:(ot + 1) * 128, :],
              waits=[("g", P.planned("g"))])
        P.step("gpsimd",
               lambda eng, wb=wb: eng.tensor_copy(out=wb[:], in_=wstage[:]),
               waits=[("dq_sync", P.qcount("sync")), ("t", P.planned("t"))],
               incs=[("g", 1)])
        return wb, P.planned("g")

    for i in range(NT):
        P.step("vector",
               lambda eng, i=i: eng.tensor_copy(out=xbf[:], in_=x_sb[i][:]),
               waits=[("dq_sync", loads_done), ("t", P.planned("t"))],
               incs=[("v", 1)])
        pieces = [xbf[:, kc * 128:(kc + 1) * 128] for kc in range(KC)]

        def evac(idx, pap, tgt, i=i):
            P.step("vector",
                   lambda eng, idx=idx, pap=pap, i=i: eng.tensor_copy(
                       out=xF[:, idx * T + i * 128: idx * T + (i + 1) * 128], in_=pap),
                   waits=[("t", tgt)], incs=[("v", 1)])

        transpose_seq(P, pieces, [tp_a, tp_b], sident,
                      pre_waits=[("v", P.planned("v"))], evac=evac)
    xf_v = P.planned("v")
    if stop == "l_xf":
        P.step("vector", None, waits=[("v", xf_v)])
        _dump(P, dbgh, xF[:], KC * T, x_sb, x3); return nc

    kdma = [0, 0]
    def project(ot):
        wb, wg = load_wblock(wqkv, ot)
        for b in range(B):
            for kc in range(KC):
                P.step("tensor",
                       lambda eng, kc=kc, b=b, wb=wb: eng.matmul(
                           out=pj_ps[:],
                           lhsT=wb[:, kc * 128:(kc + 1) * 128],
                           rhs=xF[:, kc * T + b * TB: kc * T + (b + 1) * TB],
                           start=(kc == 0), stop=(kc == KC - 1)),
                       waits=([("g", wg), ("v", xf_v), ("v", P.planned("v"))]
                              if kc == 0 else []),
                       incs=([("t", 1)] if kc == KC - 1 else []))
            yield b

    for ot in range(NH):
        for b in project(ot):
            P.step("vector",
                   lambda eng, ot=ot, b=b: eng.tensor_scalar(
                       out=qbf[:, ot * T + b * TB: ot * T + (b + 1) * TB],
                       in0=pj_ps[:], scalar1=INVSQ, scalar2=None, op0=ALU.mult),
                   waits=[("t", P.planned("t"))], incs=[("v", 1)])
    for ot in range(NH):
        for b in project(NH + ot):
            j = (ot * B + b) % 2
            P.step("vector",
                   lambda eng, j=j: eng.tensor_copy(out=kst[j][:], in_=pj_ps[:]),
                   waits=[("t", P.planned("t")), ("dq_sync", kdma[j])],
                   incs=[("v", 1)])
            row0 = (b * NH + ot) * 128
            kdma[j] = P.dma("sync", kgin.ap()[row0:row0 + 128, :], kst[j][:],
                            waits=[("v", P.planned("v"))])
    for ot in range(NH):
        for b in project(2 * NH + ot):
            P.step("vector",
                   lambda eng, ot=ot, b=b: eng.tensor_copy(
                       out=vloc[:, ot * T + b * TB: ot * T + (b + 1) * TB],
                       in_=pj_ps[:]),
                   waits=[("t", P.planned("t"))], incs=[("v", 1)])
    if stop == "l_q":
        P.step("vector", None, waits=[("v", P.planned("v"))])
        _dump(P, dbgh, qbf[:], NH * T, x_sb, x3); return nc
    if stop == "l_v":
        P.step("vector", None, waits=[("v", P.planned("v"))])
        _dump(P, dbgh, vloc[:], NH * T, x_sb, x3); return nc

    P.step("gpsimd",
           lambda eng: eng.collective_compute(
               "AllGather", ALU.bypass, replica_groups=[list(range(NCORE))],
               ins=[kgin.ap()], outs=[kg.ap()]),
           waits=[("dq_sync", P.qcount("sync"))], incs=[("cc", 1)])
    cc_done = P.planned("cc")

    bank_s = [0, 0]
    ev_v = [0, 0]
    csb = [tf_a, tf_b]
    for b in range(B):
        for hh in range(NH):
            kbase = (b * NH + hh) * 128 * TB
            kap = bass.AP(kg, kbase, [[TB, 128], [B * NH * 128 * TB, NCORE], [1, TB]])
            P.dma("sync", ksb[:].rearrange("p (r t) -> p r t", r=NCORE), kap,
                  waits=[("cc", cc_done), ("t", P.planned("t"))])
            kv_ld = P.qcount("sync")
            cs_t_prev = P.planned("t")
            for qt in range(4):
                qoff = hh * T + b * TB + qt * 128
                for ch in range(8):
                    j = ch % 2
                    w = [("s", bank_s[j])]
                    if qt == 0 and ch == 0:
                        w.append(("dq_sync", kv_ld))
                        if (b, hh) == (0, 0):
                            w.append(("v", P.planned("v")))
                    P.step("tensor",
                           lambda eng, j=j, ch=ch, qoff=qoff: eng.matmul(
                               out=sc[j][:],
                               lhsT=qbf[:, qoff:qoff + 128],
                               rhs=ksb[:, ch * 512:(ch + 1) * 512],
                               start=True, stop=True),
                           waits=w, incs=[("t", 1)])
                    P.step("scalar",
                           lambda eng, j=j, qt=qt, ch=ch: eng.activation(
                               out=probs[:, qt * S + ch * 512: qt * S + (ch + 1) * 512],
                               in_=sc[j][:], func=AF.Exp, bias=shift8[:],
                               accum_out=ssum8[:, qt * 8 + ch: qt * 8 + ch + 1]),
                           waits=[("t", P.planned("t"))], incs=[("s", 1)])
                    bank_s[j] = P.planned("s")
                P.step("scalar", lambda eng: eng_spacer(eng), incs=[("s", 1)])
                P.step("vector",
                       lambda eng, qt=qt: eng.reduce_sum(
                           out=sA[:], in_=ssum8[:, qt * 8:(qt + 1) * 8], axis=AX.X),
                       waits=[("s", P.planned("s"))])
                P.step("vector",
                       lambda eng: eng.tensor_copy(out=flush[:, 192:256],
                                                   in_=flush[:, 192:256]))
                P.step("vector", lambda eng: eng.reciprocal(out=sB[:], in_=sA[:]))
                P.step("vector",
                       lambda eng: eng.tensor_copy(out=flush[:, 192:256],
                                                   in_=flush[:, 192:256]))
                P.step("vector",
                       lambda eng, qt=qt: eng.tensor_copy(out=rs4b[:, qt:qt + 1],
                                                          in_=sB[:]),
                       waits=([("t", cs_t_prev)] if qt == 0 else []),
                       incs=[("v", 1)])
            if stop == "l_probs" and (b, hh) == (0, 0):
                P.step("vector", None, waits=[("s", P.planned("s"))])
                _dump(P, dbgh, probs[:], 4 * S, x_sb, x3); return nc
            rs_ready = P.planned("v")
            exp_done = P.planned("s")
            for ch in range(8):
                j = ch % 2
                for qt in range(4):
                    w = []
                    if qt == 0:
                        w = [("v", rs_ready), ("s", exp_done), ("v", ev_v[j])]
                    P.step("tensor",
                           lambda eng, j=j, ch=ch, qt=qt: eng.matmul(
                               out=csb[j][0:1, :],
                               lhsT=rs4b[:, qt:qt + 1],
                               rhs=probs[:, qt * S + ch * 512: qt * S + (ch + 1) * 512],
                               start=(qt == 0), stop=(qt == 3)),
                           waits=w, incs=([("t", 1)] if qt == 3 else []))
                cj = ch % 2
                P.step("vector",
                       lambda eng, j=j, cj=cj: eng.tensor_copy(out=cst[cj][:],
                                                               in_=csb[j][0:1, :]),
                       waits=[("t", P.planned("t")), ("dq_sync", kdma[cj])],
                       incs=[("v", 1)])
                ev_v[j] = P.planned("v")
                kdma[cj] = P.dma(
                    "sync", csin.ap()[ch:ch + 1, (b * NH + hh) * 512:(b * NH + hh + 1) * 512],
                    cst[cj][:], waits=[("v", P.planned("v"))])

    P.step("gpsimd",
           lambda eng: eng.collective_compute(
               "AllReduce", ALU.add, replica_groups=[list(range(NCORE))],
               ins=[csin.ap()], outs=[csout.ap()]),
           waits=[("dq_sync", P.qcount("sync"))], incs=[("cc", 1)])
    cs_done = P.planned("cc")

    cs8_t = 0
    for b in range(B):
        for hh in range(NH):
            P.dma("sync", cs8[:], csout.ap()[0:8, (b * NH + hh) * 512:(b * NH + hh + 1) * 512],
                  waits=[("cc", cs_done), ("t", cs8_t)])
            P.step("tensor",
                   lambda eng: eng.matmul(out=pj_ps[:], lhsT=sesel[:], rhs=cs8[:],
                                          start=True, stop=True),
                   waits=[("dq_sync", P.qcount("sync")), ("v", P.planned("v"))],
                   incs=[("t", 1)])
            cs8_t = P.planned("t")
            P.step("vector",
                   lambda eng: eng.tensor_copy(out=csT[:], in_=pj_ps[:]),
                   waits=[("t", P.planned("t"))], incs=[("v", 1)])
            P.step("vector",
                   lambda eng, hh=hh, b=b: eng.tensor_tensor(
                       out=vscl[:, hh * 512:(hh + 1) * 512],
                       in0=vloc[:, hh * T + b * TB: hh * T + (b + 1) * TB],
                       in1=csT[:], op=ALU.mult),
                   incs=[("v", 1)])
        if stop == "l_vs" and b == 0:
            P.step("vector", None, waits=[("v", P.planned("v"))])
            _dump(P, dbgh, vscl[:], NH * 512, x_sb, x3); return nc
        vs_ready = P.planned("v")
        for ot in range(KC):
            wb, wready = load_wblock(wo, ot)
            P.step("tensor", None,
                   waits=[("g", wready), ("v", vs_ready), ("v", P.planned("v"))])
            for h in range(NH):
                P.step("tensor",
                       lambda eng, h=h, wb=wb: eng.matmul(
                           out=pj_ps[:],
                           lhsT=wb[:, h * 128:(h + 1) * 128],
                           rhs=vscl[:, h * 512:(h + 1) * 512],
                           start=(h == 0), stop=(h == NH - 1)),
                       incs=([("t", 1)] if h == NH - 1 else []))
            P.step("vector",
                   lambda eng: eng.tensor_copy(out=pjf[:], in_=pj_ps[:]),
                   waits=[("t", P.planned("t"))], incs=[("v", 1)])
            if stop == "l_of" and (b, ot) == (0, 0):
                _dump(P, dbg, pjf[:], 512, x_sb, x3); return nc
            pieces = [pjf[:, jj * 128:(jj + 1) * 128] for jj in range(4)]

            def evac(idx, pap, tgt):
                P.step("vector",
                       lambda eng, idx=idx, pap=pap: eng.tensor_copy(
                           out=pjt[:, idx * 128:(idx + 1) * 128], in_=pap),
                       waits=[("t", tgt)], incs=[("v", 1)])

            transpose_seq(P, pieces, [tf_a, tf_b], sidentf,
                          pre_waits=[("v", P.planned("v"))], evac=evac)
            for jj in range(4):
                ti = b * 4 + jj
                P.step("vector",
                       lambda eng, ti=ti, jj=jj, ot=ot: eng.tensor_tensor(
                           out=x_sb[ti][:, ot * 128:(ot + 1) * 128],
                           in0=x_sb[ti][:, ot * 128:(ot + 1) * 128],
                           in1=pjt[:, jj * 128:(jj + 1) * 128],
                           op=ALU.add),
                       incs=([("v", 1)] if jj == 3 else []))

    if stop == "attn":
        _cut(P, x_sb, x3); return nc
    if stop == "l_res":
        _dump(P, dbg, x_sb[0][:], D, x_sb, x3); return nc

    ln_tag = emit_layernorm_inplace(P, x_sb, (nsum, rstd, lnsc), "ln1a")
    ln_done = P.planned(ln_tag)
    for i in range(NT):
        P.dma("sync", x3[i * 128:(i + 1) * 128, :], x_sb[i][:],
              waits=[(ln_tag, ln_done)] if i == 0 else [])
    P.step("sync", None, waits=[("dq_sync", P.qcount("sync"))])

    P.emit()
    return nc


def build_l1_ffn(stop=None):
    nc = bass.Bass("TRN2", target_bir_lowering=False, debug=False)
    I32 = mybir.dt.int32

    xs = nc.dram_tensor("xs", [T, D], F32, kind="ExternalInput").ap()
    upts = nc.dram_tensor("upts", [1024, D], F32, kind="ExternalInput").ap()
    dnts = nc.dram_tensor("dnts", [256, 4 * D], F32, kind="ExternalInput").ap()
    ubf = nc.dram_tensor("ubf", [128, 64], F32, kind="ExternalInput").ap()
    dbf = nc.dram_tensor("dbf", [128, 16], F32, kind="ExternalInput").ap()
    ident = nc.dram_tensor("ident", [128, 128], BF16, kind="ExternalInput").ap()
    identf = nc.dram_tensor("identf", [128, 128], F32, kind="ExternalInput").ap()
    x4 = nc.dram_tensor("x4out", [T, D], F32, kind="ExternalOutput").ap()

    qui = nc.dram_tensor("qui", [1024, D], BF16)
    qug = nc.dram_tensor("qug", [NCORE * 1024, D], BF16)
    qdi = nc.dram_tensor("qdi", [256, 4 * D], BF16)
    qdg = nc.dram_tensor("qdg", [NCORE * 256, 4 * D], BF16)

    P = Core(nc)
    o = [16640]
    def take(n):
        off = o[0]; o[0] += (n + 63) // 64 * 64
        return off
    x_sb = [P.sba(f"xt{i}", [128, D], F32, take(D * 4)) for i in range(NT)]
    sident = P.sba("sident", [128, 128], BF16, take(256))
    sidentf = P.sba("sidentf", [128, 128], F32, take(512))
    nsum = P.sba("nsum", [128, 1], F32, take(64))
    rstd = P.sba("rstd", [128, 1], F32, take(64))
    flush = P.sba("flush", [128, 256], BF16, take(512))
    _SPACER_TILE[0] = flush[:, 128:192]
    subf = P.sba("subf", [128, 64], F32, take(256))
    sdbf = P.sba("sdbf", [128, 16], F32, take(64))
    xF = P.sba("xF", [128, KC * T], BF16, take(KC * T * 2))
    wbuf = [P.sba(f"wbuf{i}", [128, KC * 128], BF16, take(KC * 128 * 2)) for i in range(2)]
    A = take(0)
    qt_w = P.sba("qt_w", [128, D], F32, A)
    qt_a = P.sba("qt_a", [128, D], F32, A + 8192)
    qt_u = P.sba("qt_u", [128, D], F32, A + 16384)
    mk = P.sba("mk", [128, D], F32, A + 24576)
    mki = P.sba("mki", [128, D], I32, A + 24576)
    mk2 = P.sba("mk2", [128, D], F32, A + 32768)
    mk2i = P.sba("mk2i", [128, D], I32, A + 32768)
    accA = P.sba("accA", [128, D], F32, A + 40960)
    accB = P.sba("accB", [128, D], F32, A + 49152)
    pmv = P.sba("pmv", [128, D], F32, A + 57344)
    dqb = P.sba("dqb", [128, D], BF16, A + 65536)
    pstg = P.sba("pstg", [128, D], BF16, A + 69632)
    amx = P.sba("amx", [128, 64], F32, A + 73728)
    amx2 = P.sba("amx2", [128, 64], F32, A + 73984)
    amx2i = P.sba("amx2i", [128, 64], I32, A + 73984)
    scl = P.sba("scl", [128, 64], F32, A + 74240)
    scli = P.sba("scli", [128, 64], I32, A + 74240)
    sss = P.sba("sss", [128, 64], F32, A + 74496)
    rrr = P.sba("rrr", [128, 64], F32, A + 74752)
    xbf = P.sba("xbf", [128, D], BF16, A + 75008)
    ug = P.sba("ug", [128, 64 * 512], BF16, A)
    gt = P.sba("gt", [128, 512], F32, A + 65536)
    gA = P.sba("gA", [128, 512], F32, A + 67584)
    gB = P.sba("gB", [128, 512], F32, A + 69632 + 4096)
    pjf = P.sba("pjf", [128, 512], F32, A + 77824)
    pjt = P.sba("pjt", [128, 512], F32, A + 79872)
    lnsc = P.sba("lnsc", [128, D], F32, A + 81920)
    assert A + 90112 <= 229376, f"overflow {A + 90112}"

    tp_a = P.ps("tp_a", [128, 1024], BF16)
    tp_b = P.ps("tp_b", [128, 1024], BF16)
    tf_a = P.ps("tf_a", [128, 512], F32)
    tf_b = P.ps("tf_b", [128, 512], F32)
    mm = [P.ps("mm_a", [128, 512], F32), P.ps("mm_b", [128, 512], F32)]

    for snm in ("t", "v", "s", "cc", "g"):
        P.sem(snm)

    for fn in (AF.Square, AF.Exp, AF.Ln, AF.Tanh, AF.Abs, AF.Sign):
        P.step("scalar", lambda eng, fn=fn: eng.activation(out=nsum[:], in_=nsum[:], func=fn))
    P.step("scalar", lambda eng: eng.memzero(nsum[:]), incs=[("s", 1)])

    P.dma("sync", sident[:], ident)
    P.dma("sync", sidentf[:], identf)
    P.dma("sync", subf[:], ubf)
    P.dma("sync", sdbf[:], dbf)
    for i in range(NT):
        P.dma("sync", x_sb[i][:], xs[i * 128:(i + 1) * 128, :])
    loads_done = P.qcount("sync")

    qdone_t = [0]
    def quant_unit(src_ap, dst_ap):
        P.dma("sync", qt_w[:], src_ap, waits=[("t", qdone_t[0]), ("v", P.planned("v"))])
        ld = P.qcount("sync")
        P.step("scalar",
               lambda eng: eng.activation(out=qt_a[:], in_=qt_w[:], func=AF.Abs),
               waits=[("dq_sync", ld), ("s", P.planned("s")), ("v", P.planned("v"))])
        P.step("scalar",
               lambda eng: eng.activation(out=pmv[:], in_=qt_w[:], func=AF.Sign))
        P.step("scalar", lambda eng: eng_spacer(eng), incs=[("s", 1)])
        sv = P.planned("s")
        P.step("vector",
               lambda eng: eng.reduce_max(
                   out=amx[:], in_=qt_a[:].rearrange("p (f o) -> p f o", o=32),
                   axis=AX.X),
               waits=[("s", sv)])
        P.step("vector", lambda eng: eng.tensor_copy(out=flush[:, 192:256], in_=flush[:, 192:256]))
        P.step("vector",
               lambda eng: eng.tensor_scalar(out=amx2[:], in0=amx[:], scalar1=1e-30,
                                             scalar2=None, op0=ALU.max))
        P.step("vector", lambda eng: eng.tensor_copy(out=flush[:, 192:256], in_=flush[:, 192:256]))
        P.step("vector",
               lambda eng: eng.tensor_scalar(out=scli[:], in0=amx2i[:], scalar1=0x7F800000,
                                             scalar2=None, op0=ALU.bitwise_and))
        P.step("vector", lambda eng: eng.tensor_copy(out=flush[:, 192:256], in_=flush[:, 192:256]))
        P.step("vector",
               lambda eng: eng.tensor_scalar(out=sss[:], in0=scl[:], scalar1=0.25,
                                             scalar2=None, op0=ALU.mult))
        P.step("vector", lambda eng: eng.tensor_copy(out=flush[:, 192:256], in_=flush[:, 192:256]))
        P.step("vector", lambda eng: eng.reciprocal(out=rrr[:], in_=sss[:]))
        P.step("vector", lambda eng: eng.tensor_copy(out=flush[:, 192:256], in_=flush[:, 192:256]))
        P.step("vector",
               lambda eng: eng.tensor_tensor(
                   out=qt_u[:].rearrange("p (f o) -> p f o", o=32),
                   in0=qt_a[:].rearrange("p (f o) -> p f o", o=32),
                   in1=rrr[:].rearrange("p (f o) -> p f o", o=1).broadcast_to([128, 64, 32]),
                   op=ALU.mult))
        acc = [accA, accB]
        cur = 0
        first = True
        for m, inc in [(0.25, 0.5), (0.75, 0.5), (1.25, 0.5), (1.75, 0.5),
                       (2.5, 1.0), (3.5, 1.0), (5.0, 2.0)]:
            P.step("vector",
                   lambda eng, m=m: eng.tensor_scalar(out=mk[:], in0=qt_u[:], scalar1=-m,
                                                      scalar2=None, op0=ALU.add))
            P.step("vector",
                   lambda eng: eng.tensor_scalar(out=mk2i[:], in0=mki[:], scalar1=31,
                                                 scalar2=None, op0=ALU.logical_shift_right))
            P.step("vector", lambda eng: eng.tensor_copy(out=mk[:], in_=mk2i[:]))
            if first:
                P.step("vector",
                       lambda eng, inc=inc: eng.tensor_scalar(
                           out=acc[0][:], in0=mk[:], scalar1=-inc, scalar2=6.0,
                           op0=ALU.mult, op1=ALU.add))
                first = False
            else:
                P.step("vector",
                       lambda eng, inc=inc: eng.tensor_scalar(
                           out=mk2[:], in0=mk[:], scalar1=-inc, scalar2=None,
                           op0=ALU.mult))
                P.step("vector",
                       lambda eng, cur=cur: eng.tensor_tensor(
                           out=acc[1 - cur][:], in0=mk2[:], in1=acc[cur][:], op=ALU.add))
                cur = 1 - cur
        P.step("vector",
               lambda eng, cur=cur: eng.tensor_tensor(
                   out=qt_a[:].rearrange("p (f o) -> p f o", o=32),
                   in0=acc[cur][:].rearrange("p (f o) -> p f o", o=32),
                   in1=sss[:].rearrange("p (f o) -> p f o", o=1).broadcast_to([128, 64, 32]),
                   op=ALU.mult))
        P.step("vector",
               lambda eng: eng.tensor_tensor(out=dqb[:], in0=qt_a[:], in1=pmv[:],
                                             op=ALU.mult),
               incs=[("v", 1)])
        pieces = [dqb[:, kc * 128:(kc + 1) * 128] for kc in range(KC)]

        def evac(idx, pap, tgt):
            P.step("vector",
                   lambda eng, idx=idx, pap=pap: eng.tensor_copy(
                       out=pstg[:, idx * 128:(idx + 1) * 128], in_=pap),
                   waits=[("t", tgt)], incs=[("v", 1)])

        transpose_seq(P, pieces, [tp_a, tp_b], sident,
                      pre_waits=[("v", P.planned("v"))], evac=evac)
        qdone_t[0] = P.planned("t")
        P.dma("sync", dst_ap, pstg[:], waits=[("v", P.planned("v"))])

    for u in range(8):
        quant_unit(upts[u * 128:(u + 1) * 128, :], qui.ap()[u * 128:(u + 1) * 128, :])
    for u2 in range(2):
        for cc4 in range(4):
            quant_unit(dnts[u2 * 128:(u2 + 1) * 128, cc4 * 2048:(cc4 + 1) * 2048],
                       qdi.ap()[u2 * 128:(u2 + 1) * 128, cc4 * 2048:(cc4 + 1) * 2048])

    P.step("gpsimd",
           lambda eng: eng.collective_compute(
               "AllGather", ALU.bypass, replica_groups=[list(range(NCORE))],
               ins=[qui.ap()], outs=[qug.ap()]),
           waits=[("dq_sync", P.qcount("sync"))], incs=[("cc", 1)])
    P.step("gpsimd",
           lambda eng: eng.collective_compute(
               "AllGather", ALU.bypass, replica_groups=[list(range(NCORE))],
               ins=[qdi.ap()], outs=[qdg.ap()]),
           incs=[("cc", 1)])
    gath = P.planned("cc")

    for i in range(NT):
        P.step("vector",
               lambda eng, i=i: eng.tensor_copy(out=xbf[:], in_=x_sb[i][:]),
               waits=[("dq_sync", loads_done), ("t", P.planned("t"))],
               incs=[("v", 1)])
        pieces = [xbf[:, kc * 128:(kc + 1) * 128] for kc in range(KC)]

        def evac(idx, pap, tgt, i=i):
            P.step("vector",
                   lambda eng, idx=idx, pap=pap, i=i: eng.tensor_copy(
                       out=xF[:, idx * T + i * 128: idx * T + (i + 1) * 128], in_=pap),
                   waits=[("t", tgt)], incs=[("v", 1)])

        transpose_seq(P, pieces, [tp_a, tp_b], sident,
                      pre_waits=[("v", P.planned("v"))], evac=evac)
    xf_v = P.planned("v")

    def load_wbf(src_dram, row0, col0, ot):
        wb = wbuf[ot % 2]
        P.dma("sync", wb[:], src_dram.ap()[row0:row0 + 128, col0:col0 + 2048],
              waits=[("g", P.planned("g")), ("cc", gath), ("t", P.planned("t"))])
        P.step("sync", None, waits=[("dq_sync", P.qcount("sync"))], incs=[("g", 1)])
        return wb, P.planned("g")

    bank_v = [0, 0]
    for b in range(B):
        for nb in range(64):
            wb, wg = load_wbf(qug, nb * 128, 0, nb)
            for kc in range(KC):
                P.step("tensor",
                       lambda eng, kc=kc, nb=nb, b=b, wb=wb: eng.matmul(
                           out=mm[nb % 2][:],
                           lhsT=wb[:, kc * 128:(kc + 1) * 128],
                           rhs=xF[:, kc * T + b * TB: kc * T + (b + 1) * TB],
                           start=(kc == 0), stop=(kc == KC - 1)),
                       waits=([("g", wg), ("v", xf_v), ("v", bank_v[nb % 2])]
                              if kc == 0 else []),
                       incs=([("t", 1)] if kc == KC - 1 else []))
            P.step("vector",
                   lambda eng, nb=nb: eng.tensor_tensor(
                       out=gt[:], in0=mm[nb % 2][:],
                       in1=subf[:, nb:nb + 1].broadcast_to([128, 512]), op=ALU.add),
                   waits=[("t", P.planned("t")), ("s", P.planned("s"))],
                   incs=[("v", 1)])
            bank_v[nb % 2] = P.planned("v")
            P.step("scalar",
                   lambda eng: eng.activation(out=gA[:], in_=gt[:], func=AF.Square),
                   waits=[("v", P.planned("v"))])
            P.step("scalar", lambda eng: eng_spacer(eng), incs=[("s", 1)])
            P.step("vector",
                   lambda eng: eng.tensor_scalar(out=gB[:], in0=gA[:], scalar1=0.044715,
                                                 scalar2=1.0, op0=ALU.mult, op1=ALU.add),
                   waits=[("s", P.planned("s"))])
            P.step("vector",
                   lambda eng: eng.tensor_tensor(out=gA[:], in0=gB[:], in1=gt[:],
                                                 op=ALU.mult))
            P.step("vector",
                   lambda eng: eng.tensor_scalar(out=gB[:], in0=gA[:],
                                                 scalar1=0.7978845608028654,
                                                 scalar2=None, op0=ALU.mult),
                   incs=[("v", 1)])
            P.step("scalar",
                   lambda eng: eng.activation(out=gA[:], in_=gB[:], func=AF.Tanh),
                   waits=[("v", P.planned("v"))])
            P.step("scalar", lambda eng: eng_spacer(eng), incs=[("s", 1)])
            P.step("vector",
                   lambda eng: eng.tensor_scalar(out=gB[:], in0=gA[:], scalar1=0.5,
                                                 scalar2=0.5, op0=ALU.mult, op1=ALU.add),
                   waits=[("s", P.planned("s"))])
            P.step("vector",
                   lambda eng, nb=nb: eng.tensor_tensor(
                       out=ug[:, nb * 512:(nb + 1) * 512], in0=gB[:], in1=gt[:],
                       op=ALU.mult),
                   incs=[("v", 1)])
        ug_v = P.planned("v")
        for db in range(KC):
            for q4 in range(4):
                wb, wg = load_wbf(qdg, db * 128, q4 * 2048, q4)
                for i2 in range(KC):
                    kcd = q4 * KC + i2
                    P.step("tensor",
                           lambda eng, i2=i2, kcd=kcd, wb=wb: eng.matmul(
                               out=mm[0][:],
                               lhsT=wb[:, i2 * 128:(i2 + 1) * 128],
                               rhs=ug[:, kcd * 512:(kcd + 1) * 512],
                               start=(kcd == 0), stop=(kcd == 63)),
                           waits=([("g", wg), ("v", ug_v), ("v", P.planned("v"))]
                                  if kcd == 0 else ([("g", wg)] if i2 == 0 else [])),
                           incs=([("t", 1)] if i2 == KC - 1 else []))
            P.step("vector",
                   lambda eng, db=db: eng.tensor_tensor(
                       out=pjf[:], in0=mm[0][:],
                       in1=sdbf[:, db:db + 1].broadcast_to([128, 512]), op=ALU.add),
                   waits=[("t", P.planned("t"))], incs=[("v", 1)])
            pieces = [pjf[:, jj * 128:(jj + 1) * 128] for jj in range(4)]

            def evac(idx, pap, tgt):
                P.step("vector",
                       lambda eng, idx=idx, pap=pap: eng.tensor_copy(
                           out=pjt[:, idx * 128:(idx + 1) * 128], in_=pap),
                       waits=[("t", tgt)], incs=[("v", 1)])

            transpose_seq(P, pieces, [tf_a, tf_b], sidentf,
                          pre_waits=[("v", P.planned("v"))], evac=evac)
            for jj in range(4):
                ti = b * 4 + jj
                P.step("vector",
                       lambda eng, ti=ti, jj=jj, db=db: eng.tensor_tensor(
                           out=x_sb[ti][:, db * 128:(db + 1) * 128],
                           in0=x_sb[ti][:, db * 128:(db + 1) * 128],
                           in1=pjt[:, jj * 128:(jj + 1) * 128],
                           op=ALU.add),
                       incs=([("v", 1)] if jj == 3 else []))

    ln_tag = emit_layernorm_inplace(P, x_sb, (nsum, rstd, lnsc), "ln1f")
    ln_done = P.planned(ln_tag)
    for i in range(NT):
        P.dma("sync", x4[i * 128:(i + 1) * 128, :], x_sb[i][:],
              waits=[(ln_tag, ln_done)] if i == 0 else [])
    P.step("sync", None, waits=[("dq_sync", P.qcount("sync"))])
    P.emit()
    return nc


def run_l1_ffn(x, up_kernel, up_bias, down_kernel, down_bias, stop=None):
    nc = build_l1_ffn(stop)
    shards = _shard_tokens(x)
    upT = np.ascontiguousarray(np.asarray(up_kernel).T)
    dnT = np.ascontiguousarray(np.asarray(down_kernel).T)
    ub = np.ascontiguousarray(np.asarray(up_bias, np.float32).reshape(64, 128).T)
    db = np.ascontiguousarray(np.asarray(down_bias, np.float32).reshape(16, 128).T)
    in_maps = []
    for c in range(NCORE):
        in_maps.append(dict(
            xs=shards[c],
            upts=np.ascontiguousarray(upT[c * 1024:(c + 1) * 1024, :]),
            dnts=np.ascontiguousarray(dnT[c * 256:(c + 1) * 256, :]),
            ubf=ub, dbf=db, ident=IDENT_BF16, identf=IDENT_F32))
    res = run_bass_kernel_spmd(nc, in_maps, core_ids=list(range(NCORE)))
    return _unshard_tokens([np.asarray(res.results[c]["x4out"]) for c in range(NCORE)])


def _rope_tables_fm(core):
    invf = 1.0 / (10000.0 ** (np.arange(0, HD, 2, dtype=np.float64) / HD))
    pos = np.arange(core * TB, (core + 1) * TB, dtype=np.float64)
    fr = pos[:, None] * invf[None, :]
    emb = np.concatenate([fr, fr], axis=1)
    cos = np.cos(emb).astype(np.float32).T
    sin = np.sin(emb).astype(np.float32).T
    cos = np.ascontiguousarray(np.concatenate([cos, cos], axis=1))
    sin = np.ascontiguousarray(np.concatenate([sin, sin], axis=1))
    return cos, sin


def _shard_tokens(x):
    x = np.asarray(x)
    return [np.ascontiguousarray(
        np.concatenate([x[b, c * TB:(c + 1) * TB, :] for b in range(B)], axis=0))
        for c in range(NCORE)]


def _unshard_tokens(shards):
    x = np.empty((B, S, D), dtype=np.asarray(shards[0]).dtype)
    for c in range(NCORE):
        for b in range(B):
            x[b, c * TB:(c + 1) * TB, :] = shards[c][b * TB:(b + 1) * TB, :]
    return x


IDENT_BF16 = np.eye(128, dtype=BF16NP)
IDENT_F32 = np.eye(128, dtype=np.float32)


def _prep_w(W):
    W = np.asarray(W)
    Din, Nout = W.shape
    kc = Din // 128
    nb = Nout // 128
    return np.ascontiguousarray(
        W.reshape(kc, 128, nb, 128).transpose(2, 1, 0, 3).reshape(nb * 128, kc * 128))


def run_l0_attn(x, Wq, Wk, Wv, Wo, stop=None):
    nc = build_l0_attn(stop)
    shards = _shard_tokens(x)
    wql, wkl, wvl, wol = _prep_w(Wq), _prep_w(Wk), _prep_w(Wv), _prep_w(Wo)
    in_maps = []
    for c in range(NCORE):
        cos, sin = _rope_tables_fm(c)
        in_maps.append(dict(xs=shards[c], wq=wql, wk=wkl, wv=wvl, wo=wol,
                            cosf=cos, sinf=sin, ident=IDENT_BF16,
                            identf=IDENT_F32))
    res = run_bass_kernel_spmd(nc, in_maps, core_ids=list(range(NCORE)))
    return _unshard_tokens([np.asarray(res.results[c]["x1out"]) for c in range(NCORE)])


def _pad_cols(a, n=128):
    out = np.zeros((a.shape[0], n), np.float32)
    out[:, :a.shape[1]] = a
    return out


def _pad_rows(a, n=128):
    out = np.zeros((n, a.shape[1]), np.float32)
    out[:a.shape[0], :] = a
    return out


def _causal_masks(core):
    out = np.zeros((4 * 128, S), np.float32)
    for qt in range(4):
        qpos = core * TB + qt * 128 + np.arange(128)
        kpos = np.arange(S)
        out[qt * 128:(qt + 1) * 128, :] = np.where(
            kpos[None, :] <= qpos[:, None], 0.0, -1e9)
    return out.astype(BF16NP)


ONES_ROW = np.ones((1, 128), dtype=np.float32)


def run_l0_ffn(h, W, stop=None):
    nc = build_l0_ffn(stop)
    shards = _shard_tokens(h)
    wqp = _prep_w(np.ascontiguousarray(np.asarray(W["wq"]).T))
    wkp = _prep_w(np.ascontiguousarray(np.asarray(W["wk"]).T))
    wvp = _prep_w(np.ascontiguousarray(np.asarray(W["wv"]).T))
    wop = _prep_w(np.ascontiguousarray(np.asarray(W["wo"]).T))
    wf1p = _prep_w(np.ascontiguousarray(np.asarray(W["w_ff1"]).T))
    wf2p = _prep_w(np.ascontiguousarray(np.asarray(W["w_ff2"]).T))
    wrp = _prep_w(_pad_cols(np.ascontiguousarray(np.asarray(W["w_router"]).T)))
    wdp = _prep_w(_pad_cols(np.ascontiguousarray(np.asarray(W["w_down"]).T)))
    wup = _prep_w(_pad_rows(np.ascontiguousarray(np.asarray(W["w_up"]).T)))
    row = lambda a: np.ascontiguousarray(np.asarray(a, np.float32).reshape(1, -1))
    in_maps = []
    for c in range(NCORE):
        in_maps.append(dict(
            hs=shards[c], wq=wqp, wk=wkp, wv=wvp, wo=wop, wf1=wf1p, wf2=wf2p,
            w_router=wrp, w_down=wdp, w_up=wup,
            sbias=row(W["simple_bias"]), g1=row(W["ln1_scale"]), b1=row(W["ln1_bias"]),
            g2=row(W["ln2_scale"]), b2=row(W["ln2_bias"]),
            cmask=_causal_masks(c), onesf=ONES_ROW,
            ident=IDENT_BF16, identf=IDENT_F32))
    res = run_bass_kernel_spmd(nc, in_maps, core_ids=list(range(NCORE)))
    return _unshard_tokens([np.asarray(res.results[c]["h2out"]) for c in range(NCORE)])


def run_l1_attn(x, w_qkv, w_o, stop=None):
    nc = build_l1_attn(stop)
    shards = _shard_tokens(x)
    wqkvp = _prep_w(np.ascontiguousarray(np.asarray(w_qkv)))
    wop = _prep_w(np.ascontiguousarray(np.asarray(w_o)))
    in_maps = []
    for c in range(NCORE):
        es = np.zeros((8, 128), np.float32)
        es[c, :] = 1.0
        in_maps.append(dict(xs=shards[c], wqkv=wqkvp, wo=wop, esel=es,
                            ident=IDENT_BF16, identf=IDENT_F32))
    res = run_bass_kernel_spmd(nc, in_maps, core_ids=list(range(NCORE)))
    return _unshard_tokens([np.asarray(res.results[c]["x3out"]) for c in range(NCORE)])


def kernel(**inputs):
    x1 = run_l0_attn(inputs["L0_attn_x"], inputs["L0_attn_Wq"], inputs["L0_attn_Wk"],
                     inputs["L0_attn_Wv"], inputs["L0_attn_Wo"])
    W0 = {n: inputs["L0_ffn_" + n] for n in ("simple_bias", "w_down", "w_up", "w_router",
          "ln1_scale", "ln1_bias", "wq", "wk", "wv", "wo", "ln2_scale", "ln2_bias",
          "w_ff1", "w_ff2")}
    x2 = run_l0_ffn(x1, W0)
    x3 = run_l1_attn(x2, inputs["L1_attn_w_qkv"], inputs["L1_attn_w_o"])
    x4 = run_l1_ffn(x3, inputs["L1_ffn_up_kernel"], inputs["L1_ffn_up_bias"],
                    inputs["L1_ffn_down_kernel"], inputs["L1_ffn_down_bias"])
    return x4
